# version 14 (final)
# baseline (speedup 1.0000x reference)
import numpy as np
import contextlib
import concourse.bass as bass
import concourse.mybir as mybir
from concourse.bass_utils import run_bass_kernel_spmd
from concourse.bass_core_rust import IndirectOffsetOnAxis

F32 = mybir.dt.float32
BF16 = mybir.dt.bfloat16
AF = mybir.ActivationFunctionType
ALU = mybir.AluOpType
AX = mybir.AxisListType

L = 2048
DM = 1024
NH = 16
DH = 64
NE = 8
DF = 4096
NT = L // 128
ECH = DM // 128

ENGINES = ("sync", "gpsimd", "tensor", "scalar", "vector")


class Sched:
    def __init__(self):
        self.ops = {e: [] for e in ENGINES}
        self.cnt = {}
        self.order = []
        self.waited = {e: {} for e in ENGINES}

    def sem(self, name):
        if name not in self.cnt:
            self.cnt[name] = 0
            self.order.append(name)
        return name

    def emit(self, eng, fn, waits=(), inc=None, inc_amt=1):
        rw = []
        for wv in waits:
            if wv is None:
                continue
            s, v = wv
            if v is None or v <= 0:
                continue
            if v > self.waited[eng].get(s, 0):
                rw.append((s, v))
                self.waited[eng][s] = v
        tok = None
        if inc is not None:
            self.sem(inc)
            self.cnt[inc] += inc_amt
            tok = (inc, self.cnt[inc])
        self.ops[eng].append((fn, rw, (inc, inc_amt) if inc else None))
        return tok

    def barrier(self):
        w = [(s, self.cnt[s]) for s in self.order]
        for e in ENGINES:
            self.emit(e, None, waits=w)


def build(cfg=None):
    cfg = cfg or {}
    nheads = cfg.get("nheads", NH)
    do_attn = cfg.get("do_attn", True)
    do_qkv = cfg.get("do_qkv", True)
    do_consts = cfg.get("do_consts", True)
    do_ln = cfg.get("do_ln", True)
    do_ffn = cfg.get("do_ffn", True)
    dumps = dict(cfg.get("dumps", ()))

    nc = bass.Bass("TRN2", target_bir_lowering=False, debug=False)

    x_d = nc.dram_tensor("x", [L, DM], F32, kind="ExternalInput").ap()
    wqkvT_d = nc.dram_tensor("wqkvT", [DM, 3 * DM], F32, kind="ExternalInput").ap()
    bqkv_d = nc.dram_tensor("bqkv", [3 * DM], F32, kind="ExternalInput").ap()
    woutT_d = nc.dram_tensor("woutT", [DM, DM], F32, kind="ExternalInput").ap()
    outb_d = nc.dram_tensor("outb", [DM], F32, kind="ExternalInput").ap()
    relpeT_d = nc.dram_tensor("relpeT", [DH, 129], F32, kind="ExternalInput").ap()
    wr_d = nc.dram_tensor("wr", [DM, NE], F32, kind="ExternalInput").ap()
    br_d = nc.dram_tensor("br", [NE], F32, kind="ExternalInput").ap()
    w1_d = nc.dram_tensor("w1", [NE, DM, DF], F32, kind="ExternalInput").ap()
    b1_d = nc.dram_tensor("b1", [NE, DF], F32, kind="ExternalInput").ap()
    w2_d = nc.dram_tensor("w2", [NE, DF, DM], F32, kind="ExternalInput").ap()
    b2_d = nc.dram_tensor("b2", [NE, DM], F32, kind="ExternalInput").ap()
    out_d = nc.dram_tensor("out", [L, DM], F32, kind="ExternalOutput").ap()

    NPAD = 4
    pad_d = nc.dram_tensor("pad_scratch", [NPAD, 128, 511], F32, kind="Internal").ap()
    x2_d = nc.dram_tensor("x2_scratch", [L, DM], F32, kind="Internal").ap()
    q_dram = nc.dram_tensor("q_scratch", [NH, 64, L], BF16, kind="Internal").ap()
    k_dram = nc.dram_tensor("k_scratch", [NH, 64, L], BF16, kind="Internal").ap()
    v_dram = nc.dram_tensor("v_scratch", [NH, 128, NT * 64], BF16, kind="Internal").ap()
    o_dram = nc.dram_tensor("o_scratch", [NH, 64, L], BF16, kind="Internal").ap()
    x2ln_dram = nc.dram_tensor("x2ln_scratch", [L, DM], BF16, kind="Internal").ap()
    y_dram = nc.dram_tensor("y_scratch", [8 * 1024, DM], F32, kind="Internal").ap()
    xcomp_dram = nc.dram_tensor("xcomp_scratch", [8 * 1024, DM], BF16, kind="Internal").ap()

    dump_d = {}
    for name, shape in dumps.items():
        dump_d[name] = nc.dram_tensor("dbg_" + name, list(shape), F32,
                                      kind="ExternalOutput").ap()

    S = Sched()
    for s in ("pe", "act", "dve", "gp"):
        S.sem(s)

    ctxP = contextlib.ExitStack()
    ctxA = contextlib.ExitStack()
    ctxR = contextlib.ExitStack()
    ctxE = contextlib.ExitStack()
    ctxF = contextlib.ExitStack()

    def sbP(name, shape, dt):
        return ctxP.enter_context(nc.sbuf_tensor(name, shape, dt, side="left"))

    def sbA(name, shape, dt):
        return ctxA.enter_context(nc.sbuf_tensor(name, shape, dt, side="left"))

    def sbR(name, shape, dt):
        return ctxR.enter_context(nc.sbuf_tensor(name, shape, dt, side="right"))

    def sbE(name, shape, dt):
        return ctxE.enter_context(nc.sbuf_tensor(name, shape, dt, side="right"))

    def sbF(name, shape, dt):
        return ctxF.enter_context(nc.sbuf_tensor(name, shape, dt, side="left"))

    psf = lambda name, shape, dt: ctxP.enter_context(nc.psum_tensor(name, shape, dt))

    ident_f = sbP("ident_f", [128, 128], F32)
    ident_b = sbP("ident_b", [128, 128], BF16)
    ones_row = sbP("ones_row", [1, 128], BF16)
    iota8 = sbP("iota8", [128, 8], F32)
    w8 = sbP("w8", [128, 8], F32)
    selrows = sbP("selrows", [8, 1024], BF16)
    relpe_dup = sbP("relpe_dup", [128, 129], BF16)
    bq_sc = sbP("bq_sc", [64, 16], F32)
    bk_sb = sbP("bk_sb", [64, 16], F32)
    bv_sb = sbP("bv_sb", [128, 8], F32)
    outb_sb = sbP("outb_sb", [128, 8], F32)
    br_sb = sbP("br_sb", [8, 1], F32)
    b2f = sbP("b2f", [8, 1024], F32)
    b2_t = sbP("b2_t", [128, 64], F32)
    tri_b = sbP("tri_b", [128, 128], BF16)
    onesq_b = sbP("onesq_b", [128, 128], BF16)
    ecrow = sbP("ecrow", [128, 8], F32)
    oh_all = sbP("oh_all", [128, 128], BF16)
    dest_all = sbP("dest_all", [128, 16], mybir.dt.int32)
    ranktmp = sbP("ranktmp", [128, 20], F32)
    ranki = sbP("ranki", [128, 3], mybir.dt.int32)
    b1st = sbP("b1st", [8, 256], F32)
    b1_t = sbP("b1_t", [128, 32 * 8], F32)
    wr_f = sbP("wr_f", [128, 64], F32)
    wrhi = sbP("wrhi", [128, 64], BF16)
    wrlo = sbP("wrlo", [128, 64], BF16)
    onehot_fm = sbP("onehot_fm", [8, L], BF16)
    wideI_b = sbP("wideI_b", [128, 896], BF16)
    cst_eps = sbP("cst_eps", [128, 1], F32)
    x_tile = [sbP(f"x_tile{i}", [128, DM], F32) for i in range(2)]
    out_tile = [sbP(f"outt{i}", [128, DM], F32) for i in range(2)]
    fm = sbP("fm", [128, ECH * L], BF16)

    xsq = [sbA(f"xsq{i}", [128, DM], F32) for i in range(2)]
    stats = [sbA(f"stats{i}", [128, 8], F32) for i in range(2)]
    xln_t = [sbA(f"xln_t{i}", [128, DM], BF16) for i in range(2)]
    zrow_b = sbA("zrow_b", [128, DM], BF16)
    xlnf2 = [sbA(f"xlnf2{i}", [128, DM], F32) for i in range(2)]
    x2_tile = [sbA(f"x2t{i}", [128, DM], F32) for i in range(2)]
    lofm_r = sbA("lofm_r", [128, DM], BF16)
    scores_sb = sbA("scores_sb", [8, L], F32)
    ohtmp = sbA("ohtmp", [128, 24], F32)
    oh_b = sbA("oh_b", [128, 8], BF16)

    wg = [sbR(f"wg{i}", [128, 8 * 128], BF16) for i in range(2)]
    qkst = [sbR(f"qkst{i}", [128, 512], BF16) for i in range(2)]
    vst = [sbR(f"vst{i}", [128, 512], BF16) for i in range(2)]
    vtst = [sbR(f"vtst{i}", [128, 512], BF16) for i in range(2)]
    qwin = [sbR(f"qwin{i}", [128, L], BF16) for i in range(2)]
    kwin = [sbR(f"kwin{i}", [128, L], BF16) for i in range(2)]
    vwin = [sbR(f"vwin{i}", [128, NT * 65], BF16) for i in range(2)]
    rowbuf = sbR("rowbuf", [1, 6 * L + 2 * 1536], BF16)
    dall = sbR("dall", [128, 16], F32)
    rfin = sbR("rfin", [128, 16], F32)
    rowstage = [sbR(f"rowstage{i}", [16, 256], F32) for i in range(2)]
    pad_sb = [sbR(f"pad{i}", [128, 511], F32) for i in range(2)]
    skew_sb = sbR("skew_sb", [128, 32 * 384], BF16)
    qrel_sb = [sbR(f"qrel{i}", [128, 129], F32) for i in range(2)]
    rmax4 = sbR("rmax4", [128, 8], F32)
    ubuf = [sbR(f"ubuf{i}", [128, 512], BF16) for i in range(2)]
    ot_sb = [sbR(f"ot{i}", [128, 512], F32) for i in range(2)]
    rz_sb = sbR("rz_sb", [128, 2], F32)
    otm_b = [sbR(f"otm{i}", [128, 64], BF16) for i in range(2)]
    ostage = [sbR(f"ostage{i}", [64, 512], BF16) for i in range(2)]

    def negc_seg(hp):
        return rowbuf[0:1, hp * L:(hp + 1) * L]

    def d_seg(hp):
        return rowbuf[0:1, (2 + hp) * L:(3 + hp) * L]

    def cdm_seg(hp, j):
        base = 4 * L + hp * 1536 + (j - 1) * 512
        return rowbuf[0:1, base:base + 512]

    def cd_seg(hp):
        base = 4 * L + 2 * 1536 + hp * L
        return rowbuf[0:1, base:base + L]

    p_big = [psf(f"p_big{i}", [128, 512], F32) for i in range(4)]
    p_o0 = psf("p_o0", [65, 512], F32)
    p_sm = psf("p_sm", [128, 384], F32)
    p_x = p_sm[0:1, 256:384]
    p_tb = [psf(f"p_tb{i}", [128, 512], BF16) for i in range(2)]

    def T(fn, waits=(), inc="pe"):
        return S.emit("tensor", fn, waits, inc)

    def A(fn, waits=(), inc="act"):
        return S.emit("scalar", fn, waits, inc)

    def V(fn, waits=(), inc="dve"):
        return S.emit("vector", fn, waits, inc)

    def G(fn, waits=(), inc="gp"):
        return S.emit("gpsimd", fn, waits, inc)

    def DSY(chan, fn, waits=()):
        return S.emit("sync", fn, waits, inc="c_" + chan, inc_amt=16)

    def DGP(chan, fn, waits=()):
        return S.emit("gpsimd", fn, waits, inc="c_" + chan, inc_amt=16)

    def chan_total(chan):
        S.sem("c_" + chan)
        return ("c_" + chan, S.cnt["c_" + chan])

    ptb_war = [None, None]
    pbig_war = [None, None, None, None]
    po_war = [None]
    psm_war = [None]
    px_war = [None]
    ubuf_war = [None, None]
    ot_war = [None, None]
    otm_war = [None, None]
    rowbuf_war = [None, None]
    cd_war = [None, None]
    cdm_war = [None, None]
    pad_war = [None, None]
    skew_war = [None] * 32
    dall_war = [None] * NT
    rfin_war = [None] * NT
    xt_war = [None, None]
    x2t_war = [None, None]
    xlnt_last = [None, None]
    win_war = [None, None]
    st_war = [None, None]
    vst_war = [None, None]
    vtst_war = [None, None]

    def mkident(t, fill):
        tz = G(lambda g, t=t: g.memset(t[:, :], 0.0))
        return G(lambda g, t=t, fill=fill: g.affine_select(
            out=t[:, :], in_=t[:, :], compare_op=ALU.not_equal, fill=fill,
            base=0, pattern=[[-1, 128]], channel_multiplier=1), waits=[tz])

    G(lambda g: g.memset(cst_eps[:, :], 1e-5), inc=None)
    nc.const_aps.register(F32, 1e-5, cst_eps[0:128, 0:1])
    tok_identf = mkident(ident_f, 1.0)
    tok_identb = mkident(ident_b, 1.0)
    twz = G(lambda g: g.memset(wideI_b[:, :], 0.0))
    tok_wideI = G(lambda g: g.affine_select(
        out=wideI_b[:, :], in_=wideI_b[:, :], compare_op=ALU.not_equal, fill=1.0,
        base=-384, pattern=[[1, 896]], channel_multiplier=-1), waits=[twz])
    tok_ones = G(lambda g: g.memset(ones_row[:, :], 1.0))
    G(lambda g: g.iota(iota8[:, :], pattern=[[1, 8]], base=0, channel_multiplier=0,
                       allow_small_or_imprecise_dtypes=True))
    tok_iota = G(lambda g: g.iota(w8[:, :], pattern=[[-1, 8]], base=8,
                                  channel_multiplier=0,
                                  allow_small_or_imprecise_dtypes=True))
    ts0 = G(lambda g: g.memset(selrows[:, :], 1.0))
    ts1 = G(lambda g: g.affine_select(out=selrows[:, :], in_=selrows[:, :],
                                      compare_op=ALU.is_ge, fill=0.0, base=0,
                                      pattern=[[1, 1024]], channel_multiplier=-128),
            waits=[ts0])
    tok_sel = G(lambda g: g.affine_select(out=selrows[:, :], in_=selrows[:, :],
                                          compare_op=ALU.is_ge, fill=0.0, base=127,
                                          pattern=[[-1, 1024]], channel_multiplier=128),
                waits=[ts1])
    tz1 = G(lambda g: g.memset(tri_b[:, :], 1.0))
    tok_tri = G(lambda g: g.affine_select(out=tri_b[:, :], in_=tri_b[:, :],
                                          compare_op=ALU.is_ge, fill=0.0, base=-1,
                                          pattern=[[1, 128]], channel_multiplier=-1),
                waits=[tz1])
    tok_onesq = G(lambda g: g.memset(onesq_b[:, :], 1.0))
    tzr = G(lambda g: g.memset(zrow_b[:, :], 0.0))
    for r in range(64):
        DGP("xz", lambda g, r=r: g.dma_start(
            out=xcomp_dram[r * 128:(r + 1) * 128, :], in_=zrow_b[:, :]),
            waits=[tzr])
    tok_xz = ("c_xz", S.cnt["c_xz"])
    for i in range(2):
        G(lambda g, i=i: g.memset(pad_sb[i][:, :], 0.0), inc=(None if i == 0 else "gp"))
    tok_padz = ("gp", S.cnt["gp"])
    for i in range(2):
        G(lambda g, i=i: g.memset(vwin[i][:, :], 1.0), inc=(None if i == 0 else "gp"))
    tok_vwin1 = ("gp", S.cnt["gp"])

    if not do_consts:
        pass
    tzrp = G(lambda g: g.memset(relpe_dup[:, :], 0.0))
    DGP("sg", lambda g: g.dma_start(out=relpe_dup[0:64, :], in_=relpeT_d[:, :]),
        waits=[tzrp])
    if do_attn:
        for i in range(2):
            tw0 = G(lambda g, i=i: g.memset(qwin[i][:, :], 0.0))
            tw1 = G(lambda g, i=i: g.memset(kwin[i][:, :], 0.0))
            win_war[i] = G(lambda g, i=i: g.memset(kwin[i][64:65, :], 1.0),
                           waits=[tw0, tw1])
    for h in range(NH):
        DGP("sg", lambda g, h=h: g.dma_start(
            out=bq_sc[:, h:h + 1], in_=bqkv_d[h * 64:(h + 1) * 64].rearrange("(a b) -> a b", b=1)))
        DGP("sg", lambda g, h=h: g.dma_start(
            out=bk_sb[:, h:h + 1],
            in_=bqkv_d[DM + h * 64:DM + (h + 1) * 64].rearrange("(a b) -> a b", b=1)))
    for c in range(8):
        DGP("sg", lambda g, c=c: g.dma_start(
            out=bv_sb[:, c:c + 1],
            in_=bqkv_d[2 * DM + c * 128:2 * DM + (c + 1) * 128].rearrange("(a b) -> a b", b=1)))
        DGP("sg", lambda g, c=c: g.dma_start(
            out=outb_sb[:, c:c + 1],
            in_=outb_d[c * 128:(c + 1) * 128].rearrange("(a b) -> a b", b=1)))
    DGP("sg", lambda g: g.dma_start(out=br_sb[:, :], in_=br_d.rearrange("(a b) -> a b", b=1)))
    DGP("sg", lambda g: g.dma_start(out=b2f[:, :], in_=b2_d[:, :]))
    DGP("sg", lambda g: g.dma_start(out=wr_f[:, :].rearrange("p (ec e) -> p ec e", e=8),
                              in_=wr_d.rearrange("(ec p) e -> p ec e", p=128)))
    DGP("sg", lambda g: g.dma_start(out=wrhi[:, :].rearrange("p (ec e) -> p ec e", e=8),
                              in_=wr_d.rearrange("(ec p) e -> p ec e", p=128)))
    b1_rd = [None, None]
    for fc in range(32):
        tld = DGP("b1" + str(fc % 2), lambda g, fc=fc: g.dma_start(
            out=b1st[:, (fc % 2) * 128:(fc % 2) * 128 + 128],
            in_=b1_d[:, fc * 128:(fc + 1) * 128]),
            waits=[b1_rd[fc % 2]] if fc >= 2 else [])
        tk = T(lambda t, fc=fc: t.matmul(
            p_sm[0:128, 0:8], b1st[0:8, (fc % 2) * 128:(fc % 2) * 128 + 128],
            ident_f[0:8, 0:8], start=True, stop=True),
            waits=[tld, tok_identf, psm_war[0]])
        b1_rd[fc % 2] = tk
        psm_war[0] = V(lambda v, fc=fc: v.tensor_copy(
            b1_t[:, fc * 8:(fc + 1) * 8], p_sm[0:128, 0:8]), waits=[tk])
    for hc in range(8):
        tk = T(lambda t, hc=hc: t.matmul(
            p_sm[0:128, 0:8], b2f[0:8, hc * 128:(hc + 1) * 128],
            ident_f[0:8, 0:8], start=True, stop=True),
            waits=[("c_sg", S.cnt["c_sg"]), tok_identf, psm_war[0]])
        psm_war[0] = V(lambda v, hc=hc: v.tensor_copy(
            b2_t[:, hc * 8:(hc + 1) * 8], p_sm[0:128, 0:8]), waits=[tk])
    tok_smalldma = chan_total("sg")
    tok_b1t = ("dve", S.cnt["dve"])

    tok_ecrow = V(lambda v: v.tensor_scalar(ecrow[:, :], iota8[:, :], 8.0, None,
                                            ALU.mult), waits=[tok_iota])
    px_war[0] = ('dve', S.cnt['dve'])
    tok_bq = V(lambda v: v.tensor_scalar(bq_sc[:, :], bq_sc[:, :], 0.125, None,
                                         ALU.mult), waits=[tok_smalldma])
    tok_wrlo = V(lambda v: v.tensor_tensor(wrlo[:, :], wr_f[:, :], wrhi[:, :],
                                           ALU.subtract), waits=[tok_smalldma])

    def emit_ln(src_toks, xt, sq, st, out_bf, out_f32=None):
        w0 = [t for t in src_toks if t is not None]
        t1 = V(lambda v: v.reduce_sum(st[:, 0:1], xt[:, :], axis=AX.X), waits=w0)
        t2 = A(lambda a: a.activation(sq[:, :], xt[:, :], AF.Square), waits=w0)
        V(lambda v: v.reduce_sum(st[:, 1:2], sq[:, :], axis=AX.X), waits=[t2])
        t4 = V(lambda v: v.scalar_tensor_tensor(st[:, 2:3], st[:, 0:1], 1.0 / DM,
                                                st[:, 0:1], ALU.mult, ALU.mult),
               waits=[t1])
        t5 = V(lambda v: v.tensor_sub(st[:, 3:4], st[:, 1:2], st[:, 2:3]), waits=[t4])
        t6 = A(lambda a: a.activation(st[:, 4:5], st[:, 3:4], AF.Sqrt,
                                      bias=1e-5, scale=1.0 / DM), waits=[t5])
        t7 = V(lambda v: v.reciprocal(st[:, 5:6], st[:, 4:5]), waits=[t6])
        t8 = V(lambda v: v.tensor_scalar(st[:, 6:7], st[:, 0:1], 1.0 / DM, None,
                                         ALU.mult), waits=[t1])
        tgt = out_f32 if out_f32 is not None else out_bf
        t9 = V(lambda v, tgt=tgt: v.tensor_scalar(tgt[:, :], xt[:, :], st[:, 6:7],
                                                  st[:, 5:6], ALU.subtract, ALU.mult),
               waits=[t7, t8])
        if out_f32 is None:
            return t9, t9, t2
        t10 = V(lambda v: v.tensor_copy(out_bf[:, :], out_f32[:, :]), waits=[t9])
        return t10, t9, t2

    def emit_fm_transpose(src_bf, dst_view4, waits):
        out = []
        for half in range(2):
            tt = None
            w = list(waits) + [tok_identb, ptb_war[half]]
            for c in range(4):
                cc = half * 4 + c
                tt = T(lambda t, cc=cc, half=half, c=c: t.transpose(
                    p_tb[half][0:128, c * 128:(c + 1) * 128],
                    src_bf[:, cc * 128:(cc + 1) * 128], ident_b[:, :]),
                    waits=w if c == 0 else [])
            tk = V(lambda v, half=half: v.tensor_copy(
                dst_view4(half),
                p_tb[half][0:128, 0:512].rearrange("p (c l) -> p c l", c=4)),
                waits=[tt])
            ptb_war[half] = tk
            out.append((tt, tk))
        return out

    def fm_view(dst, i):
        def f(half):
            return dst[:, :].rearrange("p (c l) -> p c l", c=ECH)[
                :, half * 4:half * 4 + 4, i * 128:(i + 1) * 128]
        return f

    for i in range(NT if do_ln else 0):
        par = i % 2
        xl = DSY("x" + str(par), lambda sy, i=i, par=par: sy.dma_start(
            out=x_tile[par][:, :], in_=x_d[i * 128:(i + 1) * 128, :]),
            waits=(xt_war[par] or []))
        t_apply, _, t_sq = emit_ln([xl], x_tile[par], xsq[par], stats[par], xln_t[par])
        toks = emit_fm_transpose(xln_t[par], fm_view(fm, i), [t_apply])
        xt_war[par] = [t_apply, t_sq]
    tok_xlnfm = ("dve", S.cnt["dve"])

    if "xln" in dump_d:
        DGP("dbg", lambda g: g.dma_start(out=dump_d["xln"][:, :], in_=fm[:, 0:L]),
            waits=[tok_xlnfm])

    groups = [("k", h) for h in range(nheads)] + [("q", h) for h in range(nheads)] + \
             [("v", c) for c in range(8)]
    g_dma = [None] * len(groups)
    g_lastmm = [None] * len(groups)

    def gcols(kind, idx):
        if kind == "q":
            return idx * 64, 64
        if kind == "k":
            return DM + idx * 64, 64
        return 2 * DM + idx * 128, 128

    if not do_qkv:
        groups = []
    groups = groups[:cfg.get("qkv_groups", len(groups))]
    for gi, (kind, idx) in enumerate(groups):
        c0, wdt = gcols(kind, idx)
        gp_ = gi % 2
        g_dma[gi] = DGP("wg" + str(gp_), lambda g, c0=c0, wdt=wdt, gp_=gp_: g.dma_start(
            out=wg[gp_][:, 0:8 * wdt].rearrange("p (ec w) -> p ec w", w=wdt),
            in_=wqkvT_d[:, c0:c0 + wdt].rearrange("(ec p) w -> p ec w", p=128)),
            waits=[g_lastmm[gi - 2] if gi >= 2 else None])
        for tt in range(4):
            pp = (gi * 4 + tt) % 4
            lastmm = None
            for ec in range(ECH):
                lastmm = T(lambda t, ec=ec, gp_=gp_, wdt=wdt, tt=tt, pp=pp: t.matmul(
                    p_big[pp][0:wdt, 0:512], wg[gp_][:, ec * wdt:(ec + 1) * wdt],
                    fm[:, ec * L + tt * 512:ec * L + tt * 512 + 512],
                    start=(ec == 0), stop=(ec == ECH - 1)),
                    waits=[g_dma[gi], tok_xlnfm, pbig_war[pp]] if ec == 0 else [])
            g_lastmm[gi] = lastmm
            spp = tt % 2
            if kind in ("q", "k"):
                bias = bq_sc if kind == "q" else bk_sb
                scale = 0.125 if kind == "q" else 1.0
                tev = A(lambda a, pp=pp, idx=idx, bias=bias, scale=scale, spp=spp: a.activation(
                    qkst[spp][0:64, :], p_big[pp][0:64, 0:512], AF.Identity,
                    bias=bias[:, idx:idx + 1], scale=scale),
                    waits=[lastmm, tok_bq, tok_smalldma, st_war[spp]])
                pbig_war[pp] = tev
                dd = q_dram if kind == "q" else k_dram
                st_war[spp] = DSY("qs" + str(spp), lambda sy, dd=dd, idx=idx, tt=tt, spp=spp: sy.dma_start(
                    out=dd[idx, :, tt * 512:(tt + 1) * 512], in_=qkst[spp][0:64, :]),
                    waits=[tev])
            else:
                vmode = cfg.get("v_mode", 0)
                tev = A(lambda a, pp=pp, idx=idx, spp=spp: a.activation(
                    vst[spp][:, :], p_big[pp][0:128, 0:512], AF.Identity,
                    bias=bv_sb[:, idx:idx + 1]),
                    waits=[lastmm, tok_smalldma, vst_war[spp]])
                pbig_war[pp] = tev
                if vmode >= 3:
                    vst_war[spp] = tev
                    continue
                ttk = None
                for kk in range(4):
                    ttk = T(lambda t, kk=kk, spp=spp: t.transpose(
                        p_tb[spp][0:128, kk * 128:(kk + 1) * 128],
                        vst[spp][0:128, kk * 128:(kk + 1) * 128],
                        ident_b[:, :]),
                        waits=[tev, ptb_war[spp], tok_identb] if kk == 0 else [])
                tvv = V(lambda v, spp=spp: v.tensor_copy(
                    vtst[spp][:, :].rearrange("p (h k c) -> p h k c", h=2, c=64),
                    p_tb[spp][0:128, 0:512].rearrange("p (k h c) -> p h k c", h=2, c=64)),
                    waits=[ttk, vtst_war[spp]])
                ptb_war[spp] = tvv
                vst_war[spp] = tvv
                if vmode >= 1:
                    vtst_war[spp] = tvv
                    continue
                for hh2 in range(2):
                    h2 = idx * 2 + hh2
                    vtst_war[spp] = DSY("vt" + str(spp), lambda sy, h2=h2, tt=tt, spp=spp, hh2=hh2: sy.dma_start(
                        out=v_dram[h2, :, tt * 256:(tt + 1) * 256],
                        in_=vtst[spp][:, hh2 * 256:(hh2 + 1) * 256]),
                        waits=[tvv])
    tok_qkv_pe = ("pe", S.cnt["pe"])
    tok_qkv_dma = [chan_total("qs0"), chan_total("qs1"), chan_total("vt0"), chan_total("vt1")]

    if "qh0" in dump_d:
        DGP("dbg", lambda g: g.dma_start(out=dump_d["qh0"][:, :], in_=q_dram[0, :, :]),
            waits=tok_qkv_dma)
    if "kh1" in dump_d:
        DGP("dbg", lambda g: g.dma_start(out=dump_d["kh1"][:, :], in_=k_dram[1, :, :]),
            waits=tok_qkv_dma)
    if "vfm" in dump_d:
        DGP("dbg", lambda g: g.dma_start(out=dump_d["vfm"][:, :], in_=v_dram[0, :, :]),
            waits=tok_qkv_dma)

    QQ = 4
    skew_tok = {}
    ostore_tok = {}
    winv_tok = [None, None]
    rmax_all = rmax4
    if do_attn:
      pa_state = {}
      rowstage_war = [None, None]
      pv_done = [None]
      ostate = {}
      exp_tok = [None, None]

      def emit_PV(h, qq, kb):
          wp = h % 2
          sp = kb % 2
          tpv = T(lambda t, h=h, kb=kb, sp=sp, wp=wp: t.matmul(
              p_o0[0:65, 0:512],
              vwin[wp][:, kb * 65:(kb + 1) * 65],
              ubuf[sp][:, :], start=(kb == 0), stop=(kb == NT - 1)),
              waits=[exp_tok[sp], winv_tok[wp], tok_vwin1] +
                    ([po_war[0]] if kb == 0 else []))
          ubuf_war[sp] = tpv
          win_war[wp] = tpv
          if kb == NT - 1:
              pv_done[0] = tpv
          return tpv

      def emit_O_step(h, qq, step):
          opar2 = qq % 2
          if step == 0:
              tev = A(lambda a, opar2=opar2: a.activation(
                  ot_sb[opar2][0:65, :], p_o0[0:65, 0:512], AF.Copy),
                  waits=[pv_done[0], ot_war[opar2]])
              po_war[0] = tev
              ostate["tev"] = tev
              return
          j = step - 1
          qb = qq * 4 + j
          sp2 = j % 2
          tot = T(lambda t, opar2=opar2, j=j, sp2=sp2: t.transpose(
              p_big[2 + sp2][0:128, 0:65],
              ot_sb[opar2][0:65, j * 128:(j + 1) * 128], ident_f[0:65, 0:65]),
              waits=[ostate["tev"], pbig_war[2 + sp2]])
          ot_war[opar2] = tot
          trz = V(lambda v, sp2=sp2: v.reciprocal(
              rz_sb[:, sp2:sp2 + 1], p_big[2 + sp2][0:128, 64:65]), waits=[tot])
          totm = V(lambda v, sp2=sp2: v.tensor_scalar(
              otm_b[sp2][:, :], p_big[2 + sp2][0:128, 0:64],
              rz_sb[:, sp2:sp2 + 1], None, ALU.mult), waits=[trz, otm_war[sp2]])
          pbig_war[2 + sp2] = totm
          tob = T(lambda t, sp2=sp2: t.transpose(
              p_tb[sp2][0:64, 0:128], otm_b[sp2][:, :], ident_b[:, :]),
              waits=[totm, ptb_war[sp2]])
          otm_war[sp2] = tob
          tofe = A(lambda a, j=j, sp2=sp2, opar2=opar2: a.activation(
              ostage[opar2][0:64, j * 128:(j + 1) * 128],
              p_tb[sp2][0:64, 0:128], AF.Copy),
              waits=[tob, ostore_tok.get((h, qq - 2)),
                     ostore_tok.get((h - 1, qq + 2))])
          ptb_war[sp2] = tofe
          if j == 3:
              ostore_tok[(h, qq)] = DSY("os" + str(opar2),
                  lambda sy, h=h, qq=qq, opar2=opar2: sy.dma_start(
                      out=o_dram[h, :, qq * 512:(qq + 1) * 512],
                      in_=ostage[opar2][0:64, :]),
                  waits=[("act", S.cnt["act"])])


      def emit_PA(h):
          wp = h % 2
          st = {"rmax": [None] * NT, "dall": [None] * NT, "wp": wp, "h": h}
          tmz = V(lambda v, wp=wp: v.memset(qwin[wp][64:65, :], 0.0),
                  waits=[win_war[wp]])
          wprev = [tmz] + tok_qkv_dma
          DSY("wn" + str(wp), lambda sy, h=h, wp=wp: sy.dma_start(
              out=qwin[wp][0:64, :], in_=q_dram[h, :, :]), waits=wprev)
          DSY("wn" + str(wp), lambda sy, h=h, wp=wp: sy.dma_start(
              out=kwin[wp][0:64, :], in_=k_dram[h, :, :]), waits=[])
          vv = vwin[wp][:, :].rearrange("p (kb c) -> p kb c", c=65)
          wtok = DSY("wn" + str(wp), lambda sy, h=h, wp=wp, vv=vv: sy.dma_start(
              out=vv[:, :, 0:64],
              in_=v_dram[h, :, :].rearrange("p (kb c) -> p kb c", c=64)),
              waits=[tok_vwin1])
          winv_tok[wp] = wtok
          st["wtok"] = wtok
          pa_state[h] = st

      def emit_PA_step(h, qb):
          st = pa_state[h]
          wp = st["wp"]
          wtok = st["wtok"]
          pb = qb % 2
          pslot = qb % 2
          tqr = T(lambda t, qb=qb, wp=wp, pb=pb: t.matmul(
              p_big[pb][0:128, 0:129], qwin[wp][:, qb * 128:(qb + 1) * 128],
              relpe_dup[:, :], start=True, stop=True),
              waits=[wtok, tok_smalldma, pbig_war[pb]])
          tqe = A(lambda a, qb=qb, pb=pb: a.activation(
              qrel_sb[qb % 2][:, :], p_big[pb][0:128, 0:129], AF.Copy),
              waits=[tqr])
          pbig_war[pb] = tqe
          tpc = V(lambda v, qb=qb, pslot=pslot: v.tensor_scalar(
              pad_sb[pslot][:, 191:320], qrel_sb[qb % 2][:, :],
              qrel_sb[qb % 2][:, 0:1], None, ALU.subtract),
              waits=[tqe, pad_war[pslot], tok_padz])
          tph = V(lambda v, pslot=pslot: v.tensor_scalar(
              pad_sb[pslot][:, 320:511], pad_sb[pslot][:, 320:511],
              0.0, pad_sb[pslot][:, 319:320], ALU.mult, ALU.add), waits=[tpc])
          tda = V(lambda v, qb=qb, pslot=pslot: v.tensor_copy(
              dall[:, qb:qb + 1], pad_sb[pslot][:, 319:320]),
              waits=[tph, dall_war[qb]])
          st["dall"][qb] = tda
          dslot = qb % 4
          prev_user = skew_tok.get((h, qb - 4)) or skew_tok.get((h - 1, qb + 12))
          tpo = DSY("pdo" + str(pslot), lambda sy, dslot=dslot, pslot=pslot: sy.dma_start(
              out=pad_d[dslot, :, :], in_=pad_sb[pslot][:, :]),
              waits=[tph, pad_war[pslot], prev_user])
          pad_war[pslot] = tpo
          flat = pad_d.rearrange("n a b -> (n a b)")
          sk_v = flat[dslot * 128 * 511 + 127: dslot * 128 * 511 + 127 + 128 * 510] \
              .rearrange("(p z) -> p z", z=510)
          sslot = (h % 2) * 16 + qb
          skew_tok[(h, qb)] = DGP("sk" + str(qb), lambda g, sk_v=sk_v, sslot=sslot: g.dma_start(
              out=skew_sb[:, sslot * 384:(sslot + 1) * 384], in_=sk_v[:, 0:384]),
              waits=[tpo, skew_war[sslot]])
          tred = None
          for ks in range(4):
              kbk = ks % 2
              tpre = T(lambda t, qb=qb, ks=ks, wp=wp, kbk=kbk: t.matmul(
                  p_big[kbk][0:128, 0:512], qwin[wp][:, qb * 128:(qb + 1) * 128],
                  kwin[wp][:, ks * 512:(ks + 1) * 512], start=True, stop=True),
                  waits=[pbig_war[kbk]])
              tred = V(lambda v, ks=ks, kbk=kbk: v.reduce_max(
                  rmax_all[:, ks:ks + 1], p_big[kbk][0:128, 0:512], axis=AX.X),
                  waits=[tpre] + ([st["rmax"][qb - 1]] if qb else []))
              pbig_war[kbk] = tred
          st["rmax"][qb] = V(lambda v, qb=qb: v.reduce_max(
              rfin[:, qb:qb + 1], rmax_all[:, 0:4], axis=AX.X),
              waits=[tred, rfin_war[qb]])

      def emit_PB_hop(h, step):
          hp = h % 2
          st = pa_state[h]
          if step == 0:
              st["tdt"] = T(lambda t: t.transpose(
                  p_sm[0:16, 0:128], dall[:, 0:16], ident_f[:, :]),
                  waits=[st["dall"][NT - 1], px_war[0]])
              for qb2 in range(NT):
                  dall_war[qb2] = st["tdt"]
          elif step == 1:
              st["tde"] = A(lambda a, hp=hp: a.activation(
                  rowstage[hp][0:16, 0:128], p_sm[0:16, 0:128], AF.Copy),
                  waits=[st["tdt"]] + (rowstage_war[hp] or []))
          elif step == 2:
              st["dma_d"] = DGP("rw" + str(hp), lambda g, hp=hp: g.dma_start(
                  out=d_seg(hp).rearrange("o (qb c) -> o qb c", c=128),
                  in_=rowstage[hp][0:16, 0:128]),
                  waits=[st["tde"], rowbuf_war[hp]])
          elif step == 3:
              st["tct"] = T(lambda t: t.transpose(
                  p_sm[0:16, 128:256], rfin[:, 0:16], ident_f[:, :]),
                  waits=[st["rmax"][NT - 1], st["tde"]])
              for qb2 in range(NT):
                  rfin_war[qb2] = st["tct"]
          elif step == 4:
              tneg = A(lambda a, hp=hp: a.activation(
                  rowstage[hp][0:16, 128:256], p_sm[0:16, 128:256],
                  AF.Copy, scale=-1.0), waits=[st["tct"]])
              st["tne"] = tneg
              px_war[0] = tneg
          else:
              st["dma_c"] = DGP("rw" + str(hp), lambda g, hp=hp: g.dma_start(
                  out=qwin[hp][64:65, :].rearrange("o (qb c) -> o qb c", c=128),
                  in_=rowstage[hp][0:16, 128:256]),
                  waits=[st["tne"]])
              rowstage_war[hp] = [st["dma_c"]]
              st["rows_done"] = ("c_rw" + str(hp), S.cnt["c_rw" + str(hp)])

      emit_PA(0)
      for qb in range(NT):
          emit_PA_step(0, qb)
      for step in range(6):
          emit_PB_hop(0, step)

      for h in range(nheads):
        wp = h % 2
        hp = h % 2
        if h + 1 < nheads:
            emit_PA(h + 1)
        tok_rows = pa_state[h]["rows_done"]
        pa_steps = iter(range(NT) if h + 1 < nheads else [])
        pb_steps = iter(range(6) if h + 1 < nheads else [])
        oq_steps = iter([])
        slot_i = 0
        for qq in range(QQ):
            cdm_toks = {}
            for j in range(1, 4):
                t1_ = V(lambda v, hp=hp, j=j: v.memset(cdm_seg(hp, j), 0.0),
                        waits=[tok_rows, cdm_war[hp]])
                cdm_toks[j] = V(lambda v, hp=hp, qq=qq, j=j: v.tensor_copy(
                    cdm_seg(hp, j)[0:1, 0:j * 128],
                    d_seg(hp)[0:1, qq * 512:qq * 512 + j * 128]),
                    waits=[t1_])
            prev_pv = None
            for kb in range(NT):
                sp = kb % 2
                sbank = 2 + sp
                group = []
                group.append((
                    lambda t, stt, spp, kb=kb, qq=qq, wp=wp, sbank=sbank: t.matmul(
                        p_big[sbank][0:128, 0:512],
                        kwin[wp][:, kb * 128:(kb + 1) * 128],
                        qwin[wp][:, qq * 512:(qq + 1) * 512],
                        start=stt, stop=spp),
                    [winv_tok[wp], pbig_war[sbank], tok_rows]))
                nhi = min(max((kb - 1) * 128 - qq * 512, 0), 512)
                if nhi == 512:
                    rrow = lambda hp=hp, qq=qq: d_seg(hp)[0:1, qq * 512:(qq + 1) * 512]
                    rw = [tok_rows]
                elif nhi > 0:
                    jj = nhi // 128
                    rrow = lambda hp=hp, jj=jj: cdm_seg(hp, jj)
                    rw = [cdm_toks[jj]]
                else:
                    rrow = None
                    rw = None
                if rrow is not None:
                    group.append((
                        lambda t, stt, spp, rrow=rrow, sbank=sbank: t.matmul(
                            p_big[sbank][0:128, 0:512], ones_row[0:1, :], rrow(),
                            start=stt, stop=spp),
                        rw + [tok_ones]))
                for dq in (-1, 0, 1):
                    qb = kb - dq
                    if qb < qq * 4 or qb >= (qq + 1) * 4 or qb < 0 or qb >= NT:
                        continue
                    jj = qb - qq * 4
                    sslot = (h % 2) * 16 + qb
                    group.append((
                        lambda t, stt, spp, sslot=sslot, dq=dq, jj=jj, sbank=sbank: t.matmul(
                            p_big[sbank][0:128, 0:512],
                            skew_sb[:, sslot * 384 + (dq + 1) * 128:sslot * 384 + (dq + 2) * 128],
                            wideI_b[:, 384 - 128 * jj:384 - 128 * jj + 512],
                            start=stt, stop=spp),
                        [skew_tok[(h, qb)], tok_wideI]))
                lastmm = None
                for gi2, (gfn, gw) in enumerate(group):
                    stt = (gi2 == 0)
                    spp = (gi2 == len(group) - 1)
                    lastmm = T(lambda t, gfn=gfn, stt=stt, spp=spp: gfn(t, stt, spp),
                               waits=gw or [])
                for dq in (-1, 0, 1):
                    qb = kb - dq
                    if qq * 4 <= qb < (qq + 1) * 4 and 0 <= qb < NT:
                        skew_war[(h % 2) * 16 + qb] = lastmm
                rowbuf_war[hp] = lastmm
                cd_war[hp] = lastmm
                cdm_war[hp] = lastmm
                texp = A(lambda a, sp=sp, sbank=sbank: a.activation(
                    ubuf[sp][:, :], p_big[sbank][0:128, 0:512], AF.Exp),
                    waits=[lastmm, ubuf_war[sp]])
                pbig_war[sbank] = texp
                exp_tok[kb % 2] = texp
                if kb > 0:
                    prev_pv = emit_PV(h, qq, kb - 1)
                slot_i += 1
                if slot_i > 4:
                    nx0 = next(pa_steps, None)
                    if nx0 is not None:
                        emit_PA_step(h + 1, nx0)
                    else:
                        nx = next(pb_steps, None)
                        if nx is not None:
                            emit_PB_hop(h + 1, nx)
                onx = next(oq_steps, None)
                if onx is not None:
                    emit_O_step(h, qq - 1, onx)
            prev_pv = emit_PV(h, qq, NT - 1)
            for onx in oq_steps:
                emit_O_step(h, qq - 1, onx)
            oq_steps = iter(range(5))
        for onx in oq_steps:
            emit_O_step(h, QQ - 1, onx)
    tok_attn_dve = ("dve", S.cnt["dve"])
    tok_o_dma = [("c_os0", S.cnt.get("c_os0", 0)), ("c_os1", S.cnt.get("c_os1", 0))]

    if "ofm0" in dump_d:
        DGP("dbg", lambda g: g.dma_start(out=dump_d["ofm0"][:, :], in_=o_dram[0, :, :]),
            waits=tok_o_dma)

    S.barrier()
    ctxR.close()
    woutT_hs = sbE("woutT_hs", [128, (NH // 2) * DM], BF16)
    owin = [sbE(f"owin{i}", [128, (NH // 2) * 512], BF16) for i in range(2)]
    afm_sb = [sbE(f"afm{i}", [128, 512], F32) for i in range(8)]

    for h2 in range(NH):
        DGP("sg", lambda g, h2=h2: g.dma_start(
            out=woutT_hs[(h2 % 2) * 64:(h2 % 2) * 64 + 64,
                         (h2 // 2) * DM:(h2 // 2 + 1) * DM],
            in_=woutT_d[h2 * 64:(h2 + 1) * 64, :]))
    tok_wout = chan_total("sg")
    owin_tok = [None, None]
    owin_war = [None, None]

    scores_tok = [None] * 4
    if do_attn:
      afm_tok = [None] * 8
      r_done = []
      for tt in range(4):
        tpar = tt % 2
        DSY("ow" + str(tpar), lambda sy, tt=tt, tpar=tpar: sy.dma_start(
            out=owin[tpar][0:64, :].rearrange("p (h c) -> p h c", h=NH // 2),
            in_=o_dram[:, :, tt * 512:(tt + 1) * 512].rearrange(
                "(h2 e) p c -> p h2 e c", e=2)[:, :, 0, :]),
            waits=tok_o_dma + [owin_war[tpar]])
        owin_tok[tpar] = DSY("ow" + str(tpar), lambda sy, tt=tt, tpar=tpar: sy.dma_start(
            out=owin[tpar][64:128, :].rearrange("p (h c) -> p h c", h=NH // 2),
            in_=o_dram[:, :, tt * 512:(tt + 1) * 512].rearrange(
                "(h2 e) p c -> p h2 e c", e=2)[:, :, 1, :]),
            waits=[])
        npr = max(nheads // 2, 1)
        for oc in range(8):
            pp = 2 + oc % 2
            lastmm = None
            for pr in range(npr):
                lastmm = T(lambda t, pr=pr, oc=oc, tt=tt, pp=pp, tpar=tpar: t.matmul(
                    p_big[pp][0:128, 0:512],
                    woutT_hs[0:128, pr * DM + oc * 128:pr * DM + oc * 128 + 128],
                    owin[tpar][0:128, pr * 512:(pr + 1) * 512],
                    start=(pr == 0), stop=(pr == npr - 1)),
                    waits=[owin_tok[tpar], tok_wout, pbig_war[pp]] if pr == 0 else [])
                owin_war[tpar] = lastmm
            tafm = A(lambda a, oc=oc, pp=pp: a.activation(
                afm_sb[oc][:, :], p_big[pp][0:128, 0:512], AF.Identity,
                bias=outb_sb[:, oc:oc + 1]), waits=[lastmm, tok_smalldma])
            pbig_war[pp] = tafm
            afm_tok[oc] = tafm
        for j in range(4):
            r = tt * 4 + j
            par = r % 2
            xl = DSY("x" + str(par), lambda sy, r=r, par=par: sy.dma_start(
                out=x_tile[par][:, :], in_=x_d[r * 128:(r + 1) * 128, :]),
                waits=(xt_war[par] or []))
            addtoks = []
            for oc in range(8):
                tp = oc % 2
                ttr = T(lambda t, oc=oc, j=j, tp=tp: t.transpose(
                    p_big[tp][0:128, 0:128],
                    afm_sb[oc][:, j * 128:(j + 1) * 128], ident_f[:, :]),
                    waits=[afm_tok[oc], pbig_war[tp]])
                ta = V(lambda v, oc=oc, par=par, tp=tp: v.tensor_tensor(
                    x2_tile[par][:, oc * 128:(oc + 1) * 128],
                    p_big[tp][0:128, 0:128],
                    x_tile[par][:, oc * 128:(oc + 1) * 128], ALU.add),
                    waits=[ttr, xl] + (x2t_war[par] or []))
                pbig_war[tp] = ta
                addtoks.append(ta)
            xt_war[par] = addtoks[-1:]
            tx2s = DSY("x2s" + str(par), lambda sy, r=r, par=par: sy.dma_start(
                out=x2_d[r * 128:(r + 1) * 128, :], in_=x2_tile[par][:, :]),
                waits=addtoks)
            t_bf, t_f32, t_sq2 = emit_ln(addtoks, x2_tile[par], xsq[par], stats[par],
                                         xln_t[par], xlnf2[par])
            x2t_war[par] = [t_bf, t_sq2, tx2s]
            toks_hi = emit_fm_transpose(xln_t[par], fm_view(fm, r),
                                        [t_bf, tok_qkv_pe])
            txs = DSY("xl" + str(par), lambda sy, r=r, par=par: sy.dma_start(
                out=x2ln_dram[r * 128:(r + 1) * 128, :], in_=xln_t[par][:, :]),
                waits=[t_bf])
            tlo = V(lambda v, par=par: v.tensor_tensor(
                xsq[par][:, :], xlnf2[par][:, :], xln_t[par][:, :], ALU.subtract),
                waits=[t_bf])
            tlo_b = V(lambda v, par=par: v.tensor_copy(xln_t[par][:, :],
                                                       xsq[par][:, :]),
                      waits=[tlo, toks_hi[0][0], toks_hi[1][0], txs])
            lo_toks = emit_fm_transpose(
                xln_t[par],
                lambda half: lofm_r[:, :].rearrange("p (c l) -> p c l", c=8)[
                    :, half * 4:half * 4 + 4, :],
                [tlo_b])
            xlnt_last[par] = [lo_toks[0][0], lo_toks[1][0]]
            pp2 = 2 + r % 2
            lastmm = None
            nrmm = ECH * 3
            irm = 0
            for ec in range(ECH):
                for wt, xt_, isr in ((wrhi, fm, 0), (wrlo, fm, 0), (wrhi, lofm_r, 1)):
                    first = irm == 0
                    last = irm == nrmm - 1
                    def rmm(t, ec=ec, wt=wt, xt_=xt_, isr=isr, r=r, pp2=pp2,
                            first=first, last=last):
                        rhs = (xt_[:, ec * L + r * 128:ec * L + r * 128 + 128] if isr == 0
                               else xt_[:, ec * 128:(ec + 1) * 128])
                        return t.matmul(p_big[pp2][0:8, 0:128], wt[:, ec * 8:(ec + 1) * 8],
                                        rhs, start=first, stop=last)
                    lastmm = T(rmm, waits=[toks_hi[1][1], lo_toks[1][1], tok_wrlo,
                                           pbig_war[pp2]] if first else [])
                    irm += 1
            tsc = A(lambda a, r=r, pp2=pp2: a.activation(
                scores_sb[0:8, r * 128:(r + 1) * 128], p_big[pp2][0:8, 0:128],
                AF.Identity, bias=br_sb[0:8, 0:1]), waits=[lastmm, tok_smalldma])
            pbig_war[pp2] = tsc
            scores_tok[tt] = tsc
            r_done.append(tsc)
      tok_x2fm = ("dve", S.cnt["dve"])

      oh_war = None
      for r in range(NT):
        pp = r % 2
        ttr = T(lambda t, r=r, pp=pp: t.transpose(
            p_big[pp][0:128, 0:8], scores_sb[0:8, r * 128:(r + 1) * 128],
            ident_f[0:8, 0:8]), waits=[r_done[r], pbig_war[pp]])
        t1 = V(lambda v, pp=pp: v.reduce_max(ohtmp[:, 0:1], p_big[pp][0:128, 0:8],
                                             axis=AX.X), waits=[ttr])
        t2 = V(lambda v, pp=pp: v.tensor_scalar(ohtmp[:, 8:16], p_big[pp][0:128, 0:8],
                                                ohtmp[:, 0:1], None, ALU.is_ge),
               waits=[t1])
        pbig_war[pp] = t2
        t3 = V(lambda v: v.tensor_tensor(ohtmp[:, 16:24], ohtmp[:, 8:16], w8[:, :],
                                         ALU.mult), waits=[t2, tok_iota])
        t4 = V(lambda v: v.reduce_max(ohtmp[:, 1:2], ohtmp[:, 16:24], axis=AX.X),
               waits=[t3])
        t5 = V(lambda v: v.tensor_scalar(ohtmp[:, 2:3], ohtmp[:, 1:2], -1.0, 8.0,
                                         ALU.mult, ALU.add), waits=[t4])
        t6 = V(lambda v: v.tensor_scalar(oh_b[:, :], iota8[:, :], ohtmp[:, 2:3], None,
                                         ALU.is_equal), waits=[t5, oh_war])
        V(lambda v, r=r: v.tensor_copy(oh_all[:, r * 8:(r + 1) * 8], oh_b[:, :]),
          waits=[t6])
        tt2 = T(lambda t, pp=pp: t.transpose(p_tb[pp][0:8, 0:128], oh_b[:, :],
                                             ident_b[:, :]),
                waits=[t6, ptb_war[pp]])
        oh_war = tt2
        ptb_war[pp] = V(lambda v, r=r, pp=pp: v.tensor_copy(
            onehot_fm[0:8, r * 128:(r + 1) * 128], p_tb[pp][0:8, 0:128]), waits=[tt2])
      tok_router = ("dve", S.cnt["dve"])
    else:
      tok_x2fm = ("dve", S.cnt["dve"])
      tok_router = tok_x2fm

    if do_attn:
      CAP = 1024
      for r in range(NT):
        pp = 2 + r % 2
        lastmm = None
        for r2 in range(r + 1):
            lastmm = T(lambda t, r=r, r2=r2, pp=pp: t.matmul(
                p_big[pp][0:128, 0:8],
                tri_b[:, :] if r2 == r else onesq_b[:, :],
                oh_all[:, r2 * 8:(r2 + 1) * 8],
                start=(r2 == 0), stop=(r2 == r)),
                waits=[tok_router, tok_tri, tok_onesq, pbig_war[pp]] if r2 == 0 else [])
        tb1 = V(lambda v, r=r, pp=pp: v.tensor_tensor(
            ranktmp[:, 0:8], p_big[pp][0:128, 0:8], oh_all[:, r * 8:(r + 1) * 8],
            ALU.mult), waits=[lastmm])
        pbig_war[pp] = tb1
        tb2 = V(lambda v: v.reduce_sum(ranktmp[:, 8:9], ranktmp[:, 0:8], axis=AX.X),
                waits=[tb1])
        te1 = V(lambda v, r=r: v.tensor_tensor(
            ranktmp[:, 0:8], ecrow[:, :], oh_all[:, r * 8:(r + 1) * 8], ALU.mult),
            waits=[tb2, tok_ecrow])
        te2 = V(lambda v: v.reduce_sum(ranktmp[:, 9:10], ranktmp[:, 0:8], axis=AX.X),
                waits=[te1])
        tc1 = V(lambda v: v.tensor_copy(ranki[:, 0:1], ranktmp[:, 8:9]), waits=[tb2])
        tc2 = V(lambda v: v.tensor_scalar(ranki[:, 1:2], ranki[:, 0:1], 7, None,
                                          ALU.arith_shift_right), waits=[tc1])
        tc3 = V(lambda v: v.tensor_scalar(ranki[:, 2:3], ranki[:, 0:1], 10, None,
                                          ALU.arith_shift_right), waits=[tc1])
        tc4 = V(lambda v: v.tensor_copy(ranktmp[:, 10:11], ranki[:, 1:2]), waits=[tc2])
        tc5 = V(lambda v: v.tensor_copy(ranktmp[:, 11:12], ranki[:, 2:3]), waits=[tc3])
        tm1 = V(lambda v: v.scalar_tensor_tensor(ranktmp[:, 12:13], ranktmp[:, 10:11],
                                                 -128.0, ranktmp[:, 8:9],
                                                 ALU.mult, ALU.add), waits=[tc4])
        tq1 = V(lambda v: v.tensor_tensor(ranktmp[:, 13:14], ranktmp[:, 10:11],
                                          ranktmp[:, 9:10], ALU.add),
                waits=[tc4, te2])
        td1 = V(lambda v: v.scalar_tensor_tensor(ranktmp[:, 14:15], ranktmp[:, 12:13],
                                                 64.0, ranktmp[:, 13:14],
                                                 ALU.mult, ALU.add), waits=[tm1, tq1])
        tb4 = V(lambda v: v.scalar_tensor_tensor(ranktmp[:, 19:20], ranktmp[:, 11:12],
                                                 16384.0, ranktmp[:, 14:15], ALU.mult,
                                                 ALU.add), waits=[tc5, td1])
        V(lambda v, r=r: v.tensor_copy(dest_all[:, r:r + 1], ranktmp[:, 19:20]),
          waits=[tb4])
      tok_dest = ("dve", S.cnt["dve"])
      scat_last = [None, None]
      for r in range(NT):
          par = r % 2
          w0 = ([("c_xl0", S.cnt["c_xl0"]), ("c_xl1", S.cnt["c_xl1"])]
                if r < 2 else [])
          trl = DSY("xl" + str(par), lambda sy, r=r, par=par: sy.dma_start(
              out=xln_t[par][:, :], in_=x2ln_dram[r * 128:(r + 1) * 128, :]),
              waits=w0 + (xlnt_last[par] or []) + [scat_last[par]])
          scat_last[par] = DGP("gs" + str(par), lambda g, r=r, par=par:
              g.indirect_dma_start(
                  out=xcomp_dram[:, :],
                  out_offset=IndirectOffsetOnAxis(dest_all[:, r:r + 1], 0),
                  in_=xln_t[par][:, :], in_offset=None,
                  bounds_check=8 * 1024 - 1, oob_is_err=False),
              waits=[tok_dest, trl] + ([tok_xz] if r < 2 else []))
          xlnt_last[par] = None
      tok_xcomp = [("c_gs0", S.cnt["c_gs0"]), ("c_gs1", S.cnt["c_gs1"])]
    else:
      tok_xcomp = []
      CAP = 1024

    if "x2" in dump_d:
        DSY("dbg2", lambda sy: sy.dma_start(out=dump_d["x2"][:, :], in_=x2_d[0:128, :]),
            waits=[chan_total("x2s0"), chan_total("x2s1"), tok_router])
    if "scores" in dump_d:
        DSY("dbg2", lambda sy: sy.dma_start(out=dump_d["scores"][:, :], in_=scores_sb[0:8, :]),
            waits=[tok_router])
    if "onehot" in dump_d:
        DGP("dbg", lambda g: g.dma_start(out=dump_d["onehot"][:, :], in_=onehot_fm[0:8, :]),
            waits=[tok_router])

    S.barrier()
    ctxA.close()
    ctxE.close()
    h_full = sbF("h_full", [128, 16 * 1024], BF16)
    w1o = [sbF(f"w1o{i}", [128, ECH * 512], BF16) for i in range(2)]
    w2hc = [sbF(f"w2hc{i}", [128, 16 * 128], BF16) for i in range(2)]
    xg_tm = [sbF(f"xg_tm{i}", [128, DM], BF16) for i in range(4)]
    xgfm = sbF("xgfm", [128, ECH * 1024], BF16)
    yfm = sbF("yfm", [128, 8 * 1024], F32)
    ytm = [sbF(f"ytm{i}", [128, DM], F32) for i in range(2)]
    ytile = [sbF(f"ytile{i}", [128, DM], F32) for i in range(2)]

    if do_ffn:
      w1_tok = [None, None]
      w2_tok = [None, None]
      w1_war = [None, None]
      w2_war = [None, None]
      xg_war = [None, None, None, None]
      xgfm_war = None
      yfm_war = None
      ytm_war = [None, None]
      ysc_tok = [None, None]
      for e in range(NE):
        for b in range(8):
            col = e * 8 + b
            bp = b % 4
            tg_ = DSY("gx" + str(bp), lambda sy, col=col, bp=bp: sy.dma_start(
                out=xg_tm[bp][:, :].rearrange("p (c l) -> p c l", c=1),
                in_=xcomp_dram.rearrange("(p c) l -> p c l", c=64)[
                    :, col:col + 1, :]),
                waits=tok_xcomp + [xg_war[bp]])
            toks = emit_fm_transpose(
                xg_tm[bp],
                lambda hf, b=b: xgfm[:, :].rearrange("p (c l) -> p c l", c=ECH)[
                    :, hf * 4:hf * 4 + 4, b * 128:(b + 1) * 128],
                [tg_, xgfm_war])
            xg_war[bp] = toks[-1][0]
        tok_xg = ("dve", S.cnt["dve"])
        ev0 = {}
        lastw1 = None
        for fh in range(2):
            gelu_last = None
            for f8 in range(4):
                f8g = fh * 4 + f8
                wp = f8g % 2
                w1_tok[wp] = DGP("w1" + str(wp), lambda g, e=e, f8g=f8g, wp=wp: g.dma_start(
                    out=w1o[wp][:, :].rearrange("p (ec f) -> p ec f", f=512),
                    in_=w1_d[e, :, f8g * 512:(f8g + 1) * 512]
                        .rearrange("(ec p) f -> p ec f", p=128)),
                    waits=[w1_war[wp]])
                for fc in range(4):
                    fcl = f8 * 4 + fc
                    fcg = fh * 16 + fcl
                    for sg in range(2):
                        pp = (fcl * 2 + sg) % 2
                        lastmm = None
                        for ec in range(ECH):
                            lastmm = T(lambda t, fc=fc, ec=ec, sg=sg, wp=wp, pp=pp: t.matmul(
                                p_big[pp][0:128, 0:512],
                                w1o[wp][:, ec * 512 + fc * 128:ec * 512 + fc * 128 + 128],
                                xgfm[:, ec * 1024 + sg * 512:ec * 1024 + sg * 512 + 512],
                                start=(ec == 0), stop=(ec == ECH - 1)),
                                waits=[w1_tok[wp], pbig_war[pp], tok_xg] if ec == 0 else [])
                        w1_war[wp] = lastmm
                        lastw1 = lastmm
                        tg = A(lambda a, e=e, fcg=fcg, fcl=fcl, sg=sg, pp=pp: a.activation(
                            h_full[:, fcl * 1024 + sg * 512:fcl * 1024 + sg * 512 + 512],
                            p_big[pp][0:128, 0:512],
                            AF.Gelu, bias=b1_t[:, fcg * 8 + e:fcg * 8 + e + 1]),
                            waits=[lastmm, tok_b1t, w2_war[0], w2_war[1]])
                        pbig_war[pp] = tg
                        gelu_last = tg
            for hc in range(8):
                wp2 = hc % 2
                w2_tok[wp2] = DGP("w2" + str(wp2), lambda g, e=e, fh=fh, hc=hc, wp2=wp2: g.dma_start(
                    out=w2hc[wp2][:, :].rearrange("p (fc hh) -> p fc hh", hh=128),
                    in_=w2_d[e, fh * 2048:(fh + 1) * 2048, hc * 128:(hc + 1) * 128]
                        .rearrange("(fc p) hh -> p fc hh", p=128)),
                    waits=[w2_war[wp2]])
                for sg in range(2):
                    pp2 = 2 + (hc * 2 + sg) % 2
                    lastmm = None
                    for fc16 in range(16):
                        lastmm = T(lambda t, fc16=fc16, sg=sg, wp2=wp2, pp2=pp2: t.matmul(
                            p_big[pp2][0:128, 0:512],
                            w2hc[wp2][:, fc16 * 128:(fc16 + 1) * 128],
                            h_full[:, fc16 * 1024 + sg * 512:fc16 * 1024 + sg * 512 + 512],
                            start=(fc16 == 0), stop=(fc16 == 15)),
                            waits=[w2_tok[wp2], gelu_last, pbig_war[pp2]] if fc16 == 0 else [])
                    w2_war[wp2] = lastmm
                    if fh == 0:
                        tev = A(lambda a, e=e, hc=hc, sg=sg, pp2=pp2: a.activation(
                            yfm[:, hc * 1024 + sg * 512:hc * 1024 + sg * 512 + 512],
                            p_big[pp2][0:128, 0:512],
                            AF.Identity, bias=b2_t[:, hc * 8 + e:hc * 8 + e + 1]),
                            waits=[lastmm, tok_b1t, yfm_war])
                    else:
                        tev = V(lambda v, hc=hc, sg=sg, pp2=pp2: v.tensor_tensor(
                            yfm[:, hc * 1024 + sg * 512:hc * 1024 + sg * 512 + 512],
                            yfm[:, hc * 1024 + sg * 512:hc * 1024 + sg * 512 + 512],
                            p_big[pp2][0:128, 0:512], ALU.add),
                            waits=[lastmm, ev0[(hc, sg)]])
                    ev0[(hc, sg)] = tev
                    pbig_war[pp2] = tev
        xgfm_war = lastw1
        tok_yfm = ("dve", S.cnt["dve"])
        last_ttk = None
        for b in range(8):
            col = e * 8 + b
            bp = b % 2
            evs = []
            for hg in range(2):
                ttk = None
                for hcl in range(4):
                    hc = hg * 4 + hcl
                    ttk = T(lambda t, hc=hc, hcl=hcl, b=b, bp=bp: t.transpose(
                        p_big[bp][0:128, hcl * 128:(hcl + 1) * 128],
                        yfm[:, hc * 1024 + b * 128:hc * 1024 + b * 128 + 128],
                        ident_f[:, :]),
                        waits=[tok_yfm, pbig_war[bp], tok_identf] if hcl == 0 else [])
                last_ttk = ttk
                ev = V(lambda v, hg=hg, bp=bp: v.tensor_copy(
                    ytm[bp][:, hg * 512:(hg + 1) * 512].rearrange(
                        "p (c l) -> p c l", c=4),
                    p_big[bp][0:128, 0:512].rearrange("p (c l) -> p c l", c=4)),
                    waits=[ttk, ytm_war[bp]])
                pbig_war[bp] = ev
                evs.append(ev)
            ysc = DSY("ys" + str(bp), lambda sy, col=col, bp=bp: sy.dma_start(
                out=y_dram.rearrange("(p c) l -> p c l", c=64)[:, col:col + 1, :],
                in_=ytm[bp][:, :].rearrange("p (c l) -> p c l", c=1)),
                waits=evs + [ysc_tok[bp]])
            ysc_tok[bp] = ysc
            ytm_war[bp] = ysc
        yfm_war = last_ttk
      tok_y = [("c_ys0", S.cnt["c_ys0"]), ("c_ys1", S.cnt["c_ys1"])]
    else:
      tok_y = [("dve", S.cnt["dve"])]
    fout_war = [None, None]
    yl_war = [None, None]
    for r in range(NT):
        par = r % 2
        src_final = x2_d if do_attn else x_d
        w0 = (tok_y + [chan_total("x2s0"), chan_total("x2s1")]) if r < 2 else []
        xl = DSY("x" + str(par), lambda sy, r=r, par=par, src_final=src_final: sy.dma_start(
            out=x_tile[par][:, :], in_=src_final[r * 128:(r + 1) * 128, :]),
            waits=(w0 + (xt_war[par] or [])))
        if do_ffn:
            tz = G(lambda g, par=par: g.memset(ytile[par][:, :], 0.0),
                   waits=[yl_war[par]])
            yl = DGP("yl" + str(par), lambda g, r=r, par=par: g.indirect_dma_start(
                out=ytile[par][:, :], out_offset=None,
                in_=y_dram[:, :],
                in_offset=IndirectOffsetOnAxis(dest_all[:, r:r + 1], 0),
                bounds_check=8 * 1024 - 1, oob_is_err=False),
                waits=w0 + [tz, tok_dest])
        else:
            yl = DSY("yl" + str(par), lambda sy, r=r, par=par: sy.dma_start(
                out=ytile[par][:, :], in_=y_dram[r * 128:(r + 1) * 128, :]),
                waits=[yl_war[par]])
        ta = V(lambda v, par=par: v.tensor_tensor(
            out_tile[par][:, :], ytile[par][:, :], x_tile[par][:, :], ALU.add),
            waits=[xl, yl, fout_war[par]])
        xt_war[par] = [ta]
        yl_war[par] = ta
        fout_war[par] = DSY("outs" + str(par), lambda sy, r=r, par=par: sy.dma_start(
            out=out_d[r * 128:(r + 1) * 128, :], in_=out_tile[par][:, :]),
            waits=[ta])

    S.emit("sync", None, waits=[chan_total(c) for c in
        ("outs0", "outs1", "x2s0", "x2s1", "sg", "dbg", "dbg2", "qs0", "qs1", "vt0", "vt1")
        if ("c_" + c) in S.cnt])

    semh = {}
    for name in S.order:
        semh[name] = ctxP.enter_context(nc.semaphore(name))
    with ctxP, ctxF:
        block = ctxP.enter_context(nc.Block())

        def emit_engine(section, oplist):
            def body(eng):
                for fn, waits, inc in oplist:
                    for s_, v_ in waits:
                        eng.wait_ge(semh[s_], v_)
                    if fn is None:
                        continue
                    instr = fn(eng)
                    if inc is not None:
                        instr.then_inc(semh[inc[0]], inc[1])
            section(body)

        emit_engine(block.sync, S.ops["sync"])
        emit_engine(block.gpsimd, S.ops["gpsimd"])
        emit_engine(block.tensor, S.ops["tensor"])
        emit_engine(block.scalar, S.ops["scalar"])
        emit_engine(block.vector, S.ops["vector"])
    return nc


def _host_inputs(inputs):
    q = np.ascontiguousarray(inputs["L0_attn_q"], dtype=np.float32)
    common = {
        "wqkvT": np.ascontiguousarray(np.asarray(inputs["L0_attn_in_proj_weight"], dtype=np.float32).T),
        "bqkv": np.ascontiguousarray(inputs["L0_attn_in_proj_bias"], dtype=np.float32),
        "woutT": np.ascontiguousarray(np.asarray(inputs["L0_attn_out_w"], dtype=np.float32).T),
        "outb": np.ascontiguousarray(inputs["L0_attn_out_b"], dtype=np.float32),
        "relpeT": np.ascontiguousarray(np.asarray(inputs["L0_attn_rel_pe"], dtype=np.float32).T),
        "wr": np.ascontiguousarray(inputs["L0_ffn_wr"], dtype=np.float32),
        "br": np.ascontiguousarray(inputs["L0_ffn_br"], dtype=np.float32),
        "w1": np.ascontiguousarray(inputs["L0_ffn_w1"], dtype=np.float32),
        "b1": np.ascontiguousarray(inputs["L0_ffn_b1"], dtype=np.float32),
        "w2": np.ascontiguousarray(inputs["L0_ffn_w2"], dtype=np.float32),
        "b2": np.ascontiguousarray(inputs["L0_ffn_b2"], dtype=np.float32),
    }
    in_maps = []
    for b in range(8):
        m = dict(common)
        m["x"] = np.ascontiguousarray(q[b], dtype=np.float32)
        in_maps.append(m)
    return in_maps


def kernel(**inputs):
    in_maps = _host_inputs(inputs)
    nc = build()
    res = run_bass_kernel_spmd(nc, in_maps, core_ids=list(range(8)))
    out = np.stack([np.asarray(res.results[b]["out"]) for b in range(8)], axis=0)
    return out.astype(np.float32)


# version 15
# speedup vs baseline: 1.0185x; 1.0185x over previous
import numpy as np
import contextlib
import concourse.bass as bass
import concourse.mybir as mybir
from concourse.bass_utils import run_bass_kernel_spmd
from concourse.bass_core_rust import IndirectOffsetOnAxis

F32 = mybir.dt.float32
BF16 = mybir.dt.bfloat16
AF = mybir.ActivationFunctionType
ALU = mybir.AluOpType
AX = mybir.AxisListType

L = 2048
DM = 1024
NH = 16
DH = 64
NE = 8
DF = 4096
NT = L // 128
ECH = DM // 128

ENGINES = ("sync", "gpsimd", "tensor", "scalar", "vector")


class Sched:
    def __init__(self):
        self.ops = {e: [] for e in ENGINES}
        self.cnt = {}
        self.order = []
        self.waited = {e: {} for e in ENGINES}

    def sem(self, name):
        if name not in self.cnt:
            self.cnt[name] = 0
            self.order.append(name)
        return name

    def emit(self, eng, fn, waits=(), inc=None, inc_amt=1):
        rw = []
        for wv in waits:
            if wv is None:
                continue
            s, v = wv
            if v is None or v <= 0:
                continue
            if v > self.waited[eng].get(s, 0):
                rw.append((s, v))
                self.waited[eng][s] = v
        tok = None
        if inc is not None:
            self.sem(inc)
            self.cnt[inc] += inc_amt
            tok = (inc, self.cnt[inc])
        self.ops[eng].append((fn, rw, (inc, inc_amt) if inc else None))
        return tok

    def barrier(self):
        w = [(s, self.cnt[s]) for s in self.order]
        for e in ENGINES:
            self.emit(e, None, waits=w)


def build(cfg=None):
    cfg = cfg or {}
    nheads = cfg.get("nheads", NH)
    do_attn = cfg.get("do_attn", True)
    do_qkv = cfg.get("do_qkv", True)
    do_consts = cfg.get("do_consts", True)
    do_ln = cfg.get("do_ln", True)
    do_ffn = cfg.get("do_ffn", True)
    dumps = dict(cfg.get("dumps", ()))

    nc = bass.Bass("TRN2", target_bir_lowering=False, debug=False)

    x_d = nc.dram_tensor("x", [L, DM], F32, kind="ExternalInput").ap()
    wqkvT_d = nc.dram_tensor("wqkvT", [DM, 3 * DM], F32, kind="ExternalInput").ap()
    bqkv_d = nc.dram_tensor("bqkv", [3 * DM], F32, kind="ExternalInput").ap()
    woutT_d = nc.dram_tensor("woutT", [DM, DM], F32, kind="ExternalInput").ap()
    outb_d = nc.dram_tensor("outb", [DM], F32, kind="ExternalInput").ap()
    relpeT_d = nc.dram_tensor("relpeT", [DH, 129], F32, kind="ExternalInput").ap()
    wr_d = nc.dram_tensor("wr", [DM, NE], F32, kind="ExternalInput").ap()
    br_d = nc.dram_tensor("br", [NE], F32, kind="ExternalInput").ap()
    w1_d = nc.dram_tensor("w1", [NE, DM, DF], F32, kind="ExternalInput").ap()
    b1_d = nc.dram_tensor("b1", [NE, DF], F32, kind="ExternalInput").ap()
    w2_d = nc.dram_tensor("w2", [NE, DF, DM], F32, kind="ExternalInput").ap()
    b2_d = nc.dram_tensor("b2", [NE, DM], F32, kind="ExternalInput").ap()
    out_d = nc.dram_tensor("out", [L, DM], F32, kind="ExternalOutput").ap()

    NPAD = 4
    pad_d = nc.dram_tensor("pad_scratch", [NPAD, 128, 511], F32, kind="Internal").ap()
    x2_d = nc.dram_tensor("x2_scratch", [L, DM], F32, kind="Internal").ap()
    q_dram = nc.dram_tensor("q_scratch", [NH, 64, L], BF16, kind="Internal").ap()
    k_dram = nc.dram_tensor("k_scratch", [NH, 64, L], BF16, kind="Internal").ap()
    v_dram = nc.dram_tensor("v_scratch", [NH, 128, NT * 64], BF16, kind="Internal").ap()
    o_dram = nc.dram_tensor("o_scratch", [NH, 64, L], BF16, kind="Internal").ap()
    x2ln_dram = nc.dram_tensor("x2ln_scratch", [L, DM], BF16, kind="Internal").ap()
    y_dram = nc.dram_tensor("y_scratch", [8 * 1024, DM], F32, kind="Internal").ap()
    xcomp_dram = nc.dram_tensor("xcomp_scratch", [8 * 1024, DM], BF16, kind="Internal").ap()

    dump_d = {}
    for name, shape in dumps.items():
        dump_d[name] = nc.dram_tensor("dbg_" + name, list(shape), F32,
                                      kind="ExternalOutput").ap()

    S = Sched()
    for s in ("pe", "act", "dve", "gp"):
        S.sem(s)

    ctxP = contextlib.ExitStack()
    ctxA = contextlib.ExitStack()
    ctxR = contextlib.ExitStack()
    ctxE = contextlib.ExitStack()
    ctxF = contextlib.ExitStack()

    def sbP(name, shape, dt):
        return ctxP.enter_context(nc.sbuf_tensor(name, shape, dt, side="left"))

    def sbA(name, shape, dt):
        return ctxA.enter_context(nc.sbuf_tensor(name, shape, dt, side="left"))

    def sbR(name, shape, dt):
        return ctxR.enter_context(nc.sbuf_tensor(name, shape, dt, side="right"))

    def sbE(name, shape, dt):
        return ctxE.enter_context(nc.sbuf_tensor(name, shape, dt, side="right"))

    def sbF(name, shape, dt):
        return ctxF.enter_context(nc.sbuf_tensor(name, shape, dt, side="left"))

    psf = lambda name, shape, dt: ctxP.enter_context(nc.psum_tensor(name, shape, dt))

    ident_f = sbP("ident_f", [128, 128], F32)
    ident_b = sbP("ident_b", [128, 128], BF16)
    ones_row = sbP("ones_row", [1, 128], BF16)
    iota8 = sbP("iota8", [128, 8], F32)
    w8 = sbP("w8", [128, 8], F32)
    selrows = sbP("selrows", [8, 1024], BF16)
    relpe_dup = sbP("relpe_dup", [128, 129], BF16)
    bq_sc = sbP("bq_sc", [64, 16], F32)
    bk_sb = sbP("bk_sb", [64, 16], F32)
    bv_sb = sbP("bv_sb", [128, 8], F32)
    outb_sb = sbP("outb_sb", [128, 8], F32)
    br_sb = sbP("br_sb", [8, 1], F32)
    b2f = sbP("b2f", [8, 1024], F32)
    b2_t = sbP("b2_t", [128, 64], F32)
    tri_b = sbP("tri_b", [128, 128], BF16)
    onesq_b = sbP("onesq_b", [128, 128], BF16)
    ecrow = sbP("ecrow", [128, 8], F32)
    oh_all = sbP("oh_all", [128, 128], BF16)
    dest_all = sbP("dest_all", [128, 16], mybir.dt.int32)
    ranktmp = sbP("ranktmp", [128, 20], F32)
    ranki = sbP("ranki", [128, 3], mybir.dt.int32)
    b1st = sbP("b1st", [8, 256], F32)
    b1_t = sbP("b1_t", [128, 32 * 8], F32)
    wr_f = sbP("wr_f", [128, 64], F32)
    wrhi = sbP("wrhi", [128, 64], BF16)
    wrlo = sbP("wrlo", [128, 64], BF16)
    onehot_fm = sbP("onehot_fm", [8, L], BF16)
    wideI_b = sbP("wideI_b", [128, 896], BF16)
    cst_eps = sbP("cst_eps", [128, 1], F32)
    x_tile = [sbP(f"x_tile{i}", [128, DM], F32) for i in range(2)]
    out_tile = [sbP(f"outt{i}", [128, DM], F32) for i in range(2)]
    fm = sbP("fm", [128, ECH * L], BF16)

    xsq = [sbA(f"xsq{i}", [128, DM], F32) for i in range(2)]
    stats = [sbA(f"stats{i}", [128, 8], F32) for i in range(2)]
    xln_t = [sbA(f"xln_t{i}", [128, DM], BF16) for i in range(2)]
    zrow_b = sbA("zrow_b", [128, DM], BF16)
    xlnf2 = [sbA(f"xlnf2{i}", [128, DM], F32) for i in range(2)]
    x2_tile = [sbA(f"x2t{i}", [128, DM], F32) for i in range(2)]
    lofm_r = sbA("lofm_r", [128, DM], BF16)
    scores_sb = sbA("scores_sb", [8, L], F32)
    ohtmp = sbA("ohtmp", [128, 24], F32)
    oh_b = sbA("oh_b", [128, 8], BF16)

    wg = [sbR(f"wg{i}", [128, 8 * 128], BF16) for i in range(2)]
    qkst = [sbR(f"qkst{i}", [128, 512], BF16) for i in range(2)]
    vst = [sbR(f"vst{i}", [128, 512], BF16) for i in range(2)]
    vtst = [sbR(f"vtst{i}", [128, 512], BF16) for i in range(2)]
    qwin = [sbR(f"qwin{i}", [128, L], BF16) for i in range(2)]
    kwin = [sbR(f"kwin{i}", [128, L], BF16) for i in range(2)]
    vwin = [sbR(f"vwin{i}", [128, NT * 65], BF16) for i in range(2)]
    rowbuf = sbR("rowbuf", [1, 6 * L + 2 * 1536], BF16)
    dall = sbR("dall", [128, 16], F32)
    rfin = sbR("rfin", [128, 16], F32)
    rowstage = [sbR(f"rowstage{i}", [16, 256], F32) for i in range(2)]
    pad_sb = [sbR(f"pad{i}", [128, 511], F32) for i in range(2)]
    skew_sb = sbR("skew_sb", [128, 32 * 384], BF16)
    qrel_sb = [sbR(f"qrel{i}", [128, 129], F32) for i in range(2)]
    rmax4 = sbR("rmax4", [128, 8], F32)
    ubuf = [sbR(f"ubuf{i}", [128, 512], BF16) for i in range(2)]
    ot_sb = [sbR(f"ot{i}", [128, 512], F32) for i in range(2)]
    rz_sb = sbR("rz_sb", [128, 2], F32)
    otm_b = [sbR(f"otm{i}", [128, 64], BF16) for i in range(2)]
    ostage = [sbR(f"ostage{i}", [64, 512], BF16) for i in range(2)]

    def negc_seg(hp):
        return rowbuf[0:1, hp * L:(hp + 1) * L]

    def d_seg(hp):
        return rowbuf[0:1, (2 + hp) * L:(3 + hp) * L]

    def cdm_seg(hp, j):
        base = 4 * L + hp * 1536 + (j - 1) * 512
        return rowbuf[0:1, base:base + 512]

    def cd_seg(hp):
        base = 4 * L + 2 * 1536 + hp * L
        return rowbuf[0:1, base:base + L]

    p_big = [psf(f"p_big{i}", [128, 512], F32) for i in range(4)]
    p_o0 = psf("p_o0", [65, 512], F32)
    p_sm = psf("p_sm", [128, 384], F32)
    p_x = p_sm[0:1, 256:384]
    p_tb = [psf(f"p_tb{i}", [128, 512], BF16) for i in range(2)]

    def T(fn, waits=(), inc="pe"):
        return S.emit("tensor", fn, waits, inc)

    def A(fn, waits=(), inc="act"):
        return S.emit("scalar", fn, waits, inc)

    def V(fn, waits=(), inc="dve"):
        return S.emit("vector", fn, waits, inc)

    def G(fn, waits=(), inc="gp"):
        return S.emit("gpsimd", fn, waits, inc)

    def DSY(chan, fn, waits=()):
        return S.emit("sync", fn, waits, inc="c_" + chan, inc_amt=16)

    def DGP(chan, fn, waits=()):
        return S.emit("gpsimd", fn, waits, inc="c_" + chan, inc_amt=16)

    def chan_total(chan):
        S.sem("c_" + chan)
        return ("c_" + chan, S.cnt["c_" + chan])

    ptb_war = [None, None]
    pbig_war = [None, None, None, None]
    po_war = [None]
    psm_war = [None]
    px_war = [None]
    ubuf_war = [None, None]
    ot_war = [None, None]
    otm_war = [None, None]
    rowbuf_war = [None, None]
    cd_war = [None, None]
    cdm_war = [None, None]
    pad_war = [None, None]
    skew_war = [None] * 32
    dall_war = [None] * NT
    rfin_war = [None] * NT
    xt_war = [None, None]
    x2t_war = [None, None]
    xlnt_last = [None, None]
    win_war = [None, None]
    st_war = [None, None]
    vst_war = [None, None]
    vtst_war = [None, None]

    def mkident(t, fill):
        tz = G(lambda g, t=t: g.memset(t[:, :], 0.0))
        return G(lambda g, t=t, fill=fill: g.affine_select(
            out=t[:, :], in_=t[:, :], compare_op=ALU.not_equal, fill=fill,
            base=0, pattern=[[-1, 128]], channel_multiplier=1), waits=[tz])

    G(lambda g: g.memset(cst_eps[:, :], 1e-5), inc=None)
    nc.const_aps.register(F32, 1e-5, cst_eps[0:128, 0:1])
    tok_identf = mkident(ident_f, 1.0)
    tok_identb = mkident(ident_b, 1.0)
    twz = G(lambda g: g.memset(wideI_b[:, :], 0.0))
    tok_wideI = G(lambda g: g.affine_select(
        out=wideI_b[:, :], in_=wideI_b[:, :], compare_op=ALU.not_equal, fill=1.0,
        base=-384, pattern=[[1, 896]], channel_multiplier=-1), waits=[twz])
    tok_ones = G(lambda g: g.memset(ones_row[:, :], 1.0))
    G(lambda g: g.iota(iota8[:, :], pattern=[[1, 8]], base=0, channel_multiplier=0,
                       allow_small_or_imprecise_dtypes=True))
    tok_iota = G(lambda g: g.iota(w8[:, :], pattern=[[-1, 8]], base=8,
                                  channel_multiplier=0,
                                  allow_small_or_imprecise_dtypes=True))
    ts0 = G(lambda g: g.memset(selrows[:, :], 1.0))
    ts1 = G(lambda g: g.affine_select(out=selrows[:, :], in_=selrows[:, :],
                                      compare_op=ALU.is_ge, fill=0.0, base=0,
                                      pattern=[[1, 1024]], channel_multiplier=-128),
            waits=[ts0])
    tok_sel = G(lambda g: g.affine_select(out=selrows[:, :], in_=selrows[:, :],
                                          compare_op=ALU.is_ge, fill=0.0, base=127,
                                          pattern=[[-1, 1024]], channel_multiplier=128),
                waits=[ts1])
    tz1 = G(lambda g: g.memset(tri_b[:, :], 1.0))
    tok_tri = G(lambda g: g.affine_select(out=tri_b[:, :], in_=tri_b[:, :],
                                          compare_op=ALU.is_ge, fill=0.0, base=-1,
                                          pattern=[[1, 128]], channel_multiplier=-1),
                waits=[tz1])
    tok_onesq = G(lambda g: g.memset(onesq_b[:, :], 1.0))
    tzr = G(lambda g: g.memset(zrow_b[:, :], 0.0))
    for r in range(64):
        DGP("xz", lambda g, r=r: g.dma_start(
            out=xcomp_dram[r * 128:(r + 1) * 128, :], in_=zrow_b[:, :]),
            waits=[tzr])
    tok_xz = ("c_xz", S.cnt["c_xz"])
    for i in range(2):
        G(lambda g, i=i: g.memset(pad_sb[i][:, :], 0.0), inc=(None if i == 0 else "gp"))
    tok_padz = ("gp", S.cnt["gp"])
    for i in range(2):
        G(lambda g, i=i: g.memset(vwin[i][:, :], 1.0), inc=(None if i == 0 else "gp"))
    tok_vwin1 = ("gp", S.cnt["gp"])

    if not do_consts:
        pass
    tzrp = G(lambda g: g.memset(relpe_dup[:, :], 0.0))
    DGP("sg", lambda g: g.dma_start(out=relpe_dup[0:64, :], in_=relpeT_d[:, :]),
        waits=[tzrp])
    if do_attn:
        for i in range(2):
            tw0 = G(lambda g, i=i: g.memset(qwin[i][:, :], 0.0))
            tw1 = G(lambda g, i=i: g.memset(kwin[i][:, :], 0.0))
            win_war[i] = G(lambda g, i=i: g.memset(kwin[i][64:65, :], 1.0),
                           waits=[tw0, tw1])
    for h in range(NH):
        DGP("sg", lambda g, h=h: g.dma_start(
            out=bq_sc[:, h:h + 1], in_=bqkv_d[h * 64:(h + 1) * 64].rearrange("(a b) -> a b", b=1)))
        DGP("sg", lambda g, h=h: g.dma_start(
            out=bk_sb[:, h:h + 1],
            in_=bqkv_d[DM + h * 64:DM + (h + 1) * 64].rearrange("(a b) -> a b", b=1)))
    for c in range(8):
        DGP("sg", lambda g, c=c: g.dma_start(
            out=bv_sb[:, c:c + 1],
            in_=bqkv_d[2 * DM + c * 128:2 * DM + (c + 1) * 128].rearrange("(a b) -> a b", b=1)))
        DGP("sg", lambda g, c=c: g.dma_start(
            out=outb_sb[:, c:c + 1],
            in_=outb_d[c * 128:(c + 1) * 128].rearrange("(a b) -> a b", b=1)))
    DGP("sg", lambda g: g.dma_start(out=br_sb[:, :], in_=br_d.rearrange("(a b) -> a b", b=1)))
    DGP("sg", lambda g: g.dma_start(out=b2f[:, :], in_=b2_d[:, :]))
    DGP("sg", lambda g: g.dma_start(out=wr_f[:, :].rearrange("p (ec e) -> p ec e", e=8),
                              in_=wr_d.rearrange("(ec p) e -> p ec e", p=128)))
    DGP("sg", lambda g: g.dma_start(out=wrhi[:, :].rearrange("p (ec e) -> p ec e", e=8),
                              in_=wr_d.rearrange("(ec p) e -> p ec e", p=128)))
    b1_rd = [None, None]
    for fc in range(32):
        tld = DGP("b1" + str(fc % 2), lambda g, fc=fc: g.dma_start(
            out=b1st[:, (fc % 2) * 128:(fc % 2) * 128 + 128],
            in_=b1_d[:, fc * 128:(fc + 1) * 128]),
            waits=[b1_rd[fc % 2]] if fc >= 2 else [])
        tk = T(lambda t, fc=fc: t.matmul(
            p_sm[0:128, 0:8], b1st[0:8, (fc % 2) * 128:(fc % 2) * 128 + 128],
            ident_f[0:8, 0:8], start=True, stop=True),
            waits=[tld, tok_identf, psm_war[0]])
        b1_rd[fc % 2] = tk
        psm_war[0] = V(lambda v, fc=fc: v.tensor_copy(
            b1_t[:, fc * 8:(fc + 1) * 8], p_sm[0:128, 0:8]), waits=[tk])
    for hc in range(8):
        tk = T(lambda t, hc=hc: t.matmul(
            p_sm[0:128, 0:8], b2f[0:8, hc * 128:(hc + 1) * 128],
            ident_f[0:8, 0:8], start=True, stop=True),
            waits=[("c_sg", S.cnt["c_sg"]), tok_identf, psm_war[0]])
        psm_war[0] = V(lambda v, hc=hc: v.tensor_copy(
            b2_t[:, hc * 8:(hc + 1) * 8], p_sm[0:128, 0:8]), waits=[tk])
    tok_smalldma = chan_total("sg")
    tok_b1t = ("dve", S.cnt["dve"])

    tok_ecrow = V(lambda v: v.tensor_scalar(ecrow[:, :], iota8[:, :], 8.0, None,
                                            ALU.mult), waits=[tok_iota])
    px_war[0] = ('dve', S.cnt['dve'])
    tok_bq = V(lambda v: v.tensor_scalar(bq_sc[:, :], bq_sc[:, :], 0.125, None,
                                         ALU.mult), waits=[tok_smalldma])
    tok_wrlo = V(lambda v: v.tensor_tensor(wrlo[:, :], wr_f[:, :], wrhi[:, :],
                                           ALU.subtract), waits=[tok_smalldma])

    def emit_ln(src_toks, xt, sq, st, out_bf, out_f32=None):
        w0 = [t for t in src_toks if t is not None]
        t1 = V(lambda v: v.reduce_sum(st[:, 0:1], xt[:, :], axis=AX.X), waits=w0)
        t2 = A(lambda a: a.activation(sq[:, :], xt[:, :], AF.Square), waits=w0)
        V(lambda v: v.reduce_sum(st[:, 1:2], sq[:, :], axis=AX.X), waits=[t2])
        t4 = V(lambda v: v.scalar_tensor_tensor(st[:, 2:3], st[:, 0:1], 1.0 / DM,
                                                st[:, 0:1], ALU.mult, ALU.mult),
               waits=[t1])
        t5 = V(lambda v: v.tensor_sub(st[:, 3:4], st[:, 1:2], st[:, 2:3]), waits=[t4])
        t6 = A(lambda a: a.activation(st[:, 4:5], st[:, 3:4], AF.Sqrt,
                                      bias=1e-5, scale=1.0 / DM), waits=[t5])
        t7 = V(lambda v: v.reciprocal(st[:, 5:6], st[:, 4:5]), waits=[t6])
        t8 = V(lambda v: v.tensor_scalar(st[:, 6:7], st[:, 0:1], 1.0 / DM, None,
                                         ALU.mult), waits=[t1])
        tgt = out_f32 if out_f32 is not None else out_bf
        t9 = V(lambda v, tgt=tgt: v.tensor_scalar(tgt[:, :], xt[:, :], st[:, 6:7],
                                                  st[:, 5:6], ALU.subtract, ALU.mult),
               waits=[t7, t8])
        if out_f32 is None:
            return t9, t9, t2
        t10 = V(lambda v: v.tensor_copy(out_bf[:, :], out_f32[:, :]), waits=[t9])
        return t10, t9, t2

    def emit_fm_transpose(src_bf, dst_view4, waits):
        out = []
        for half in range(2):
            tt = None
            w = list(waits) + [tok_identb, ptb_war[half]]
            for c in range(4):
                cc = half * 4 + c
                tt = T(lambda t, cc=cc, half=half, c=c: t.transpose(
                    p_tb[half][0:128, c * 128:(c + 1) * 128],
                    src_bf[:, cc * 128:(cc + 1) * 128], ident_b[:, :]),
                    waits=w if c == 0 else [])
            tk = V(lambda v, half=half: v.tensor_copy(
                dst_view4(half),
                p_tb[half][0:128, 0:512].rearrange("p (c l) -> p c l", c=4)),
                waits=[tt])
            ptb_war[half] = tk
            out.append((tt, tk))
        return out

    def fm_view(dst, i):
        def f(half):
            return dst[:, :].rearrange("p (c l) -> p c l", c=ECH)[
                :, half * 4:half * 4 + 4, i * 128:(i + 1) * 128]
        return f

    for i in range(NT if do_ln else 0):
        par = i % 2
        xl = DSY("x" + str(par), lambda sy, i=i, par=par: sy.dma_start(
            out=x_tile[par][:, :], in_=x_d[i * 128:(i + 1) * 128, :]),
            waits=(xt_war[par] or []))
        t_apply, _, t_sq = emit_ln([xl], x_tile[par], xsq[par], stats[par], xln_t[par])
        toks = emit_fm_transpose(xln_t[par], fm_view(fm, i), [t_apply])
        xt_war[par] = [t_apply, t_sq]
    tok_xlnfm = ("dve", S.cnt["dve"])

    if "xln" in dump_d:
        DGP("dbg", lambda g: g.dma_start(out=dump_d["xln"][:, :], in_=fm[:, 0:L]),
            waits=[tok_xlnfm])

    groups = [("k", h) for h in range(nheads)] + [("q", h) for h in range(nheads)] + \
             [("v", c) for c in range(8)]
    g_dma = [None] * len(groups)
    g_lastmm = [None] * len(groups)

    def gcols(kind, idx):
        if kind == "q":
            return idx * 64, 64
        if kind == "k":
            return DM + idx * 64, 64
        return 2 * DM + idx * 128, 128

    if not do_qkv:
        groups = []
    groups = groups[:cfg.get("qkv_groups", len(groups))]
    for gi, (kind, idx) in enumerate(groups):
        c0, wdt = gcols(kind, idx)
        gp_ = gi % 2
        g_dma[gi] = DGP("wg" + str(gp_), lambda g, c0=c0, wdt=wdt, gp_=gp_: g.dma_start(
            out=wg[gp_][:, 0:8 * wdt].rearrange("p (ec w) -> p ec w", w=wdt),
            in_=wqkvT_d[:, c0:c0 + wdt].rearrange("(ec p) w -> p ec w", p=128)),
            waits=[g_lastmm[gi - 2] if gi >= 2 else None])
        for tt in range(4):
            pp = (gi * 4 + tt) % 4
            lastmm = None
            for ec in range(ECH):
                lastmm = T(lambda t, ec=ec, gp_=gp_, wdt=wdt, tt=tt, pp=pp: t.matmul(
                    p_big[pp][0:wdt, 0:512], wg[gp_][:, ec * wdt:(ec + 1) * wdt],
                    fm[:, ec * L + tt * 512:ec * L + tt * 512 + 512],
                    start=(ec == 0), stop=(ec == ECH - 1)),
                    waits=[g_dma[gi], tok_xlnfm, pbig_war[pp]] if ec == 0 else [])
            g_lastmm[gi] = lastmm
            spp = tt % 2
            if kind in ("q", "k"):
                bias = bq_sc if kind == "q" else bk_sb
                scale = 0.125 if kind == "q" else 1.0
                tev = A(lambda a, pp=pp, idx=idx, bias=bias, scale=scale, spp=spp: a.activation(
                    qkst[spp][0:64, :], p_big[pp][0:64, 0:512], AF.Identity,
                    bias=bias[:, idx:idx + 1], scale=scale),
                    waits=[lastmm, tok_bq, tok_smalldma, st_war[spp]])
                pbig_war[pp] = tev
                dd = q_dram if kind == "q" else k_dram
                st_war[spp] = DSY("qs" + str(spp), lambda sy, dd=dd, idx=idx, tt=tt, spp=spp: sy.dma_start(
                    out=dd[idx, :, tt * 512:(tt + 1) * 512], in_=qkst[spp][0:64, :]),
                    waits=[tev])
            else:
                vmode = cfg.get("v_mode", 0)
                tev = A(lambda a, pp=pp, idx=idx, spp=spp: a.activation(
                    vst[spp][:, :], p_big[pp][0:128, 0:512], AF.Identity,
                    bias=bv_sb[:, idx:idx + 1]),
                    waits=[lastmm, tok_smalldma, vst_war[spp]])
                pbig_war[pp] = tev
                if vmode >= 3:
                    vst_war[spp] = tev
                    continue
                ttk = None
                for kk in range(4):
                    ttk = T(lambda t, kk=kk, spp=spp: t.transpose(
                        p_tb[spp][0:128, kk * 128:(kk + 1) * 128],
                        vst[spp][0:128, kk * 128:(kk + 1) * 128],
                        ident_b[:, :]),
                        waits=[tev, ptb_war[spp], tok_identb] if kk == 0 else [])
                tvv = V(lambda v, spp=spp: v.tensor_copy(
                    vtst[spp][:, :].rearrange("p (h k c) -> p h k c", h=2, c=64),
                    p_tb[spp][0:128, 0:512].rearrange("p (k h c) -> p h k c", h=2, c=64)),
                    waits=[ttk, vtst_war[spp]])
                ptb_war[spp] = tvv
                vst_war[spp] = tvv
                if vmode >= 1:
                    vtst_war[spp] = tvv
                    continue
                for hh2 in range(2):
                    h2 = idx * 2 + hh2
                    vtst_war[spp] = DSY("vt" + str(spp), lambda sy, h2=h2, tt=tt, spp=spp, hh2=hh2: sy.dma_start(
                        out=v_dram[h2, :, tt * 256:(tt + 1) * 256],
                        in_=vtst[spp][:, hh2 * 256:(hh2 + 1) * 256]),
                        waits=[tvv])
    tok_qkv_pe = ("pe", S.cnt["pe"])
    tok_qkv_dma = [chan_total("qs0"), chan_total("qs1"), chan_total("vt0"), chan_total("vt1")]

    if "qh0" in dump_d:
        DGP("dbg", lambda g: g.dma_start(out=dump_d["qh0"][:, :], in_=q_dram[0, :, :]),
            waits=tok_qkv_dma)
    if "kh1" in dump_d:
        DGP("dbg", lambda g: g.dma_start(out=dump_d["kh1"][:, :], in_=k_dram[1, :, :]),
            waits=tok_qkv_dma)
    if "vfm" in dump_d:
        DGP("dbg", lambda g: g.dma_start(out=dump_d["vfm"][:, :], in_=v_dram[0, :, :]),
            waits=tok_qkv_dma)

    QQ = 4
    skew_tok = {}
    ostore_tok = {}
    winv_tok = [None, None]
    rmax_all = rmax4
    if do_attn:
      pa_state = {}
      rowstage_war = [None, None]
      pv_done = [None]
      ostate = {}
      exp_tok = [None, None]

      def emit_PV(h, qq, kb):
          wp = h % 2
          sp = kb % 2
          tpv = T(lambda t, h=h, kb=kb, sp=sp, wp=wp: t.matmul(
              p_o0[0:65, 0:512],
              vwin[wp][:, kb * 65:(kb + 1) * 65],
              ubuf[sp][:, :], start=(kb == 0), stop=(kb == NT - 1)),
              waits=[exp_tok[sp], winv_tok[wp], tok_vwin1] +
                    ([po_war[0]] if kb == 0 else []))
          ubuf_war[sp] = tpv
          win_war[wp] = tpv
          if kb == NT - 1:
              pv_done[0] = tpv
          return tpv

      def emit_O_step(h, qq, step):
          opar2 = qq % 2
          if step == 0:
              tev = A(lambda a, opar2=opar2: a.activation(
                  ot_sb[opar2][0:65, :], p_o0[0:65, 0:512], AF.Copy),
                  waits=[pv_done[0], ot_war[opar2]])
              po_war[0] = tev
              ostate["tev"] = tev
              return
          j = step - 1
          qb = qq * 4 + j
          sp2 = j % 2
          tot = T(lambda t, opar2=opar2, j=j, sp2=sp2: t.transpose(
              p_big[2 + sp2][0:128, 0:65],
              ot_sb[opar2][0:65, j * 128:(j + 1) * 128], ident_f[0:65, 0:65]),
              waits=[ostate["tev"], pbig_war[2 + sp2]])
          ot_war[opar2] = tot
          trz = V(lambda v, sp2=sp2: v.reciprocal(
              rz_sb[:, sp2:sp2 + 1], p_big[2 + sp2][0:128, 64:65]), waits=[tot])
          totm = V(lambda v, sp2=sp2: v.tensor_scalar(
              otm_b[sp2][:, :], p_big[2 + sp2][0:128, 0:64],
              rz_sb[:, sp2:sp2 + 1], None, ALU.mult), waits=[trz, otm_war[sp2]])
          pbig_war[2 + sp2] = totm
          tob = T(lambda t, sp2=sp2: t.transpose(
              p_tb[sp2][0:64, 0:128], otm_b[sp2][:, :], ident_b[:, :]),
              waits=[totm, ptb_war[sp2]])
          otm_war[sp2] = tob
          tofe = A(lambda a, j=j, sp2=sp2, opar2=opar2: a.activation(
              ostage[opar2][0:64, j * 128:(j + 1) * 128],
              p_tb[sp2][0:64, 0:128], AF.Copy),
              waits=[tob, ostore_tok.get((h, qq - 2)),
                     ostore_tok.get((h - 1, qq + 2))])
          ptb_war[sp2] = tofe
          if j == 3:
              ostore_tok[(h, qq)] = DSY("os" + str(opar2),
                  lambda sy, h=h, qq=qq, opar2=opar2: sy.dma_start(
                      out=o_dram[h, :, qq * 512:(qq + 1) * 512],
                      in_=ostage[opar2][0:64, :]),
                  waits=[("act", S.cnt["act"])])


      def emit_PA(h):
          wp = h % 2
          st = {"rmax": [None] * NT, "dall": [None] * NT, "wp": wp, "h": h}
          tmz = V(lambda v, wp=wp: v.memset(qwin[wp][64:65, :], 0.0),
                  waits=[win_war[wp]])
          wprev = [tmz] + tok_qkv_dma
          DSY("wn" + str(wp), lambda sy, h=h, wp=wp: sy.dma_start(
              out=qwin[wp][0:64, :], in_=q_dram[h, :, :]), waits=wprev)
          kt = DSY("wn" + str(wp), lambda sy, h=h, wp=wp: sy.dma_start(
              out=kwin[wp][0:64, :], in_=k_dram[h, :, :]), waits=[])
          vv = vwin[wp][:, :].rearrange("p (kb c) -> p kb c", c=65)
          wtok = DGP("wv" + str(wp), lambda g, h=h, wp=wp, vv=vv: g.dma_start(
              out=vv[:, :, 0:64],
              in_=v_dram[h, :, :].rearrange("p (kb c) -> p kb c", c=64)),
              waits=[tok_vwin1, win_war[wp]])
          winv_tok[wp] = wtok
          st["wtok"] = kt
          pa_state[h] = st

      def emit_PA_step(h, qb):
          st = pa_state[h]
          wp = st["wp"]
          wtok = st["wtok"]
          pb = qb % 2
          pslot = qb % 2
          tqr = T(lambda t, qb=qb, wp=wp, pb=pb: t.matmul(
              p_big[pb][0:128, 0:129], qwin[wp][:, qb * 128:(qb + 1) * 128],
              relpe_dup[:, :], start=True, stop=True),
              waits=[wtok, tok_smalldma, pbig_war[pb]])
          tqe = A(lambda a, qb=qb, pb=pb: a.activation(
              qrel_sb[qb % 2][:, :], p_big[pb][0:128, 0:129], AF.Copy),
              waits=[tqr])
          pbig_war[pb] = tqe
          tpc = V(lambda v, qb=qb, pslot=pslot: v.tensor_scalar(
              pad_sb[pslot][:, 191:320], qrel_sb[qb % 2][:, :],
              qrel_sb[qb % 2][:, 0:1], None, ALU.subtract),
              waits=[tqe, pad_war[pslot], tok_padz])
          tph = V(lambda v, pslot=pslot: v.tensor_scalar(
              pad_sb[pslot][:, 320:511], pad_sb[pslot][:, 320:511],
              0.0, pad_sb[pslot][:, 319:320], ALU.mult, ALU.add), waits=[tpc])
          tda = V(lambda v, qb=qb, pslot=pslot: v.tensor_copy(
              dall[:, qb:qb + 1], pad_sb[pslot][:, 319:320]),
              waits=[tph, dall_war[qb]])
          st["dall"][qb] = tda
          dslot = qb % 4
          prev_user = skew_tok.get((h, qb - 4)) or skew_tok.get((h - 1, qb + 12))
          tpo = DSY("pdo" + str(pslot), lambda sy, dslot=dslot, pslot=pslot: sy.dma_start(
              out=pad_d[dslot, :, :], in_=pad_sb[pslot][:, :]),
              waits=[tph, pad_war[pslot], prev_user])
          pad_war[pslot] = tpo
          flat = pad_d.rearrange("n a b -> (n a b)")
          sk_v = flat[dslot * 128 * 511 + 127: dslot * 128 * 511 + 127 + 128 * 510] \
              .rearrange("(p z) -> p z", z=510)
          sslot = (h % 2) * 16 + qb
          skew_tok[(h, qb)] = DGP("sk" + str(qb), lambda g, sk_v=sk_v, sslot=sslot: g.dma_start(
              out=skew_sb[:, sslot * 384:(sslot + 1) * 384], in_=sk_v[:, 0:384]),
              waits=[tpo, skew_war[sslot]])
          tred = None
          for ks in range(4):
              kbk = ks % 2
              tpre = T(lambda t, qb=qb, ks=ks, wp=wp, kbk=kbk: t.matmul(
                  p_big[kbk][0:128, 0:512], qwin[wp][:, qb * 128:(qb + 1) * 128],
                  kwin[wp][:, ks * 512:(ks + 1) * 512], start=True, stop=True),
                  waits=[pbig_war[kbk]])
              tred = V(lambda v, ks=ks, kbk=kbk: v.reduce_max(
                  rmax_all[:, ks:ks + 1], p_big[kbk][0:128, 0:512], axis=AX.X),
                  waits=[tpre] + ([st["rmax"][qb - 1]] if qb else []))
              pbig_war[kbk] = tred
          st["rmax"][qb] = V(lambda v, qb=qb: v.reduce_max(
              rfin[:, qb:qb + 1], rmax_all[:, 0:4], axis=AX.X),
              waits=[tred, rfin_war[qb]])

      def emit_PB_hop(h, step):
          hp = h % 2
          st = pa_state[h]
          if step == 0:
              st["tdt"] = T(lambda t: t.transpose(
                  p_sm[0:16, 0:128], dall[:, 0:16], ident_f[:, :]),
                  waits=[st["dall"][NT - 1], px_war[0]])
              for qb2 in range(NT):
                  dall_war[qb2] = st["tdt"]
          elif step == 1:
              st["tde"] = A(lambda a, hp=hp: a.activation(
                  rowstage[hp][0:16, 0:128], p_sm[0:16, 0:128], AF.Copy),
                  waits=[st["tdt"]] + (rowstage_war[hp] or []))
          elif step == 2:
              st["dma_d"] = DGP("rw" + str(hp), lambda g, hp=hp: g.dma_start(
                  out=d_seg(hp).rearrange("o (qb c) -> o qb c", c=128),
                  in_=rowstage[hp][0:16, 0:128]),
                  waits=[st["tde"], rowbuf_war[hp]])
          elif step == 3:
              st["tct"] = T(lambda t: t.transpose(
                  p_sm[0:16, 128:256], rfin[:, 0:16], ident_f[:, :]),
                  waits=[st["rmax"][NT - 1], st["tde"]])
              for qb2 in range(NT):
                  rfin_war[qb2] = st["tct"]
          elif step == 4:
              tneg = A(lambda a, hp=hp: a.activation(
                  rowstage[hp][0:16, 128:256], p_sm[0:16, 128:256],
                  AF.Copy, scale=-1.0), waits=[st["tct"]])
              st["tne"] = tneg
              px_war[0] = tneg
          else:
              st["dma_c"] = DGP("rw" + str(hp), lambda g, hp=hp: g.dma_start(
                  out=qwin[hp][64:65, :].rearrange("o (qb c) -> o qb c", c=128),
                  in_=rowstage[hp][0:16, 128:256]),
                  waits=[st["tne"]])
              rowstage_war[hp] = [st["dma_c"]]
              st["rows_done"] = ("c_rw" + str(hp), S.cnt["c_rw" + str(hp)])

      emit_PA(0)
      for qb in range(NT):
          emit_PA_step(0, qb)
      for step in range(6):
          emit_PB_hop(0, step)

      for h in range(nheads):
        wp = h % 2
        hp = h % 2
        if h + 1 < nheads:
            emit_PA(h + 1)
        tok_rows = pa_state[h]["rows_done"]
        pa_steps = iter(range(NT) if h + 1 < nheads else [])
        pb_steps = iter(range(6) if h + 1 < nheads else [])
        oq_steps = iter([])
        slot_i = 0
        for qq in range(QQ):
            cdm_toks = {}
            for j in range(1, 4):
                t1_ = V(lambda v, hp=hp, j=j: v.memset(cdm_seg(hp, j), 0.0),
                        waits=[tok_rows, cdm_war[hp]])
                cdm_toks[j] = V(lambda v, hp=hp, qq=qq, j=j: v.tensor_copy(
                    cdm_seg(hp, j)[0:1, 0:j * 128],
                    d_seg(hp)[0:1, qq * 512:qq * 512 + j * 128]),
                    waits=[t1_])
            prev_pv = None
            for kb in range(NT):
                sp = kb % 2
                sbank = 2 + sp
                group = []
                group.append((
                    lambda t, stt, spp, kb=kb, qq=qq, wp=wp, sbank=sbank: t.matmul(
                        p_big[sbank][0:128, 0:512],
                        kwin[wp][:, kb * 128:(kb + 1) * 128],
                        qwin[wp][:, qq * 512:(qq + 1) * 512],
                        start=stt, stop=spp),
                    [winv_tok[wp], pbig_war[sbank], tok_rows]))
                nhi = min(max((kb - 1) * 128 - qq * 512, 0), 512)
                if nhi == 512:
                    rrow = lambda hp=hp, qq=qq: d_seg(hp)[0:1, qq * 512:(qq + 1) * 512]
                    rw = [tok_rows]
                elif nhi > 0:
                    jj = nhi // 128
                    rrow = lambda hp=hp, jj=jj: cdm_seg(hp, jj)
                    rw = [cdm_toks[jj]]
                else:
                    rrow = None
                    rw = None
                if rrow is not None:
                    group.append((
                        lambda t, stt, spp, rrow=rrow, sbank=sbank: t.matmul(
                            p_big[sbank][0:128, 0:512], ones_row[0:1, :], rrow(),
                            start=stt, stop=spp),
                        rw + [tok_ones]))
                for dq in (-1, 0, 1):
                    qb = kb - dq
                    if qb < qq * 4 or qb >= (qq + 1) * 4 or qb < 0 or qb >= NT:
                        continue
                    jj = qb - qq * 4
                    sslot = (h % 2) * 16 + qb
                    group.append((
                        lambda t, stt, spp, sslot=sslot, dq=dq, jj=jj, sbank=sbank: t.matmul(
                            p_big[sbank][0:128, 0:512],
                            skew_sb[:, sslot * 384 + (dq + 1) * 128:sslot * 384 + (dq + 2) * 128],
                            wideI_b[:, 384 - 128 * jj:384 - 128 * jj + 512],
                            start=stt, stop=spp),
                        [skew_tok[(h, qb)], tok_wideI]))
                lastmm = None
                for gi2, (gfn, gw) in enumerate(group):
                    stt = (gi2 == 0)
                    spp = (gi2 == len(group) - 1)
                    lastmm = T(lambda t, gfn=gfn, stt=stt, spp=spp: gfn(t, stt, spp),
                               waits=gw or [])
                for dq in (-1, 0, 1):
                    qb = kb - dq
                    if qq * 4 <= qb < (qq + 1) * 4 and 0 <= qb < NT:
                        skew_war[(h % 2) * 16 + qb] = lastmm
                rowbuf_war[hp] = lastmm
                cd_war[hp] = lastmm
                cdm_war[hp] = lastmm
                texp = A(lambda a, sp=sp, sbank=sbank: a.activation(
                    ubuf[sp][:, :], p_big[sbank][0:128, 0:512], AF.Exp),
                    waits=[lastmm, ubuf_war[sp]])
                pbig_war[sbank] = texp
                exp_tok[kb % 2] = texp
                if kb > 0:
                    prev_pv = emit_PV(h, qq, kb - 1)
                slot_i += 1
                if slot_i > 4:
                    nx0 = next(pa_steps, None)
                    if nx0 is not None:
                        emit_PA_step(h + 1, nx0)
                    else:
                        nx = next(pb_steps, None)
                        if nx is not None:
                            emit_PB_hop(h + 1, nx)
                onx = next(oq_steps, None)
                if onx is not None:
                    emit_O_step(h, qq - 1, onx)
            prev_pv = emit_PV(h, qq, NT - 1)
            for onx in oq_steps:
                emit_O_step(h, qq - 1, onx)
            oq_steps = iter(range(5))
        for onx in oq_steps:
            emit_O_step(h, QQ - 1, onx)
    tok_attn_dve = ("dve", S.cnt["dve"])
    tok_o_dma = [("c_os0", S.cnt.get("c_os0", 0)), ("c_os1", S.cnt.get("c_os1", 0))]

    if "ofm0" in dump_d:
        DGP("dbg", lambda g: g.dma_start(out=dump_d["ofm0"][:, :], in_=o_dram[0, :, :]),
            waits=tok_o_dma)

    S.barrier()
    ctxR.close()
    woutT_hs = sbE("woutT_hs", [128, (NH // 2) * DM], BF16)
    owin = [sbE(f"owin{i}", [128, (NH // 2) * 512], BF16) for i in range(2)]
    afm_sb = [sbE(f"afm{i}", [128, 512], F32) for i in range(8)]

    for h2 in range(NH):
        DGP("sg", lambda g, h2=h2: g.dma_start(
            out=woutT_hs[(h2 % 2) * 64:(h2 % 2) * 64 + 64,
                         (h2 // 2) * DM:(h2 // 2 + 1) * DM],
            in_=woutT_d[h2 * 64:(h2 + 1) * 64, :]))
    tok_wout = chan_total("sg")
    owin_tok = [None, None]
    owin_war = [None, None]

    scores_tok = [None] * 4
    if do_attn:
      afm_tok = [None] * 8
      r_done = []
      for tt in range(4):
        tpar = tt % 2
        DSY("ow" + str(tpar), lambda sy, tt=tt, tpar=tpar: sy.dma_start(
            out=owin[tpar][0:64, :].rearrange("p (h c) -> p h c", h=NH // 2),
            in_=o_dram[:, :, tt * 512:(tt + 1) * 512].rearrange(
                "(h2 e) p c -> p h2 e c", e=2)[:, :, 0, :]),
            waits=tok_o_dma + [owin_war[tpar]])
        owin_tok[tpar] = DSY("ow" + str(tpar), lambda sy, tt=tt, tpar=tpar: sy.dma_start(
            out=owin[tpar][64:128, :].rearrange("p (h c) -> p h c", h=NH // 2),
            in_=o_dram[:, :, tt * 512:(tt + 1) * 512].rearrange(
                "(h2 e) p c -> p h2 e c", e=2)[:, :, 1, :]),
            waits=[])
        npr = max(nheads // 2, 1)
        for oc in range(8):
            pp = 2 + oc % 2
            lastmm = None
            for pr in range(npr):
                lastmm = T(lambda t, pr=pr, oc=oc, tt=tt, pp=pp, tpar=tpar: t.matmul(
                    p_big[pp][0:128, 0:512],
                    woutT_hs[0:128, pr * DM + oc * 128:pr * DM + oc * 128 + 128],
                    owin[tpar][0:128, pr * 512:(pr + 1) * 512],
                    start=(pr == 0), stop=(pr == npr - 1)),
                    waits=[owin_tok[tpar], tok_wout, pbig_war[pp]] if pr == 0 else [])
                owin_war[tpar] = lastmm
            tafm = A(lambda a, oc=oc, pp=pp: a.activation(
                afm_sb[oc][:, :], p_big[pp][0:128, 0:512], AF.Identity,
                bias=outb_sb[:, oc:oc + 1]), waits=[lastmm, tok_smalldma])
            pbig_war[pp] = tafm
            afm_tok[oc] = tafm
        for j in range(4):
            r = tt * 4 + j
            par = r % 2
            xl = DSY("x" + str(par), lambda sy, r=r, par=par: sy.dma_start(
                out=x_tile[par][:, :], in_=x_d[r * 128:(r + 1) * 128, :]),
                waits=(xt_war[par] or []))
            addtoks = []
            for oc in range(8):
                tp = oc % 2
                ttr = T(lambda t, oc=oc, j=j, tp=tp: t.transpose(
                    p_big[tp][0:128, 0:128],
                    afm_sb[oc][:, j * 128:(j + 1) * 128], ident_f[:, :]),
                    waits=[afm_tok[oc], pbig_war[tp]])
                ta = V(lambda v, oc=oc, par=par, tp=tp: v.tensor_tensor(
                    x2_tile[par][:, oc * 128:(oc + 1) * 128],
                    p_big[tp][0:128, 0:128],
                    x_tile[par][:, oc * 128:(oc + 1) * 128], ALU.add),
                    waits=[ttr, xl] + (x2t_war[par] or []))
                pbig_war[tp] = ta
                addtoks.append(ta)
            xt_war[par] = addtoks[-1:]
            tx2s = DSY("x2s" + str(par), lambda sy, r=r, par=par: sy.dma_start(
                out=x2_d[r * 128:(r + 1) * 128, :], in_=x2_tile[par][:, :]),
                waits=addtoks)
            t_bf, t_f32, t_sq2 = emit_ln(addtoks, x2_tile[par], xsq[par], stats[par],
                                         xln_t[par], xlnf2[par])
            x2t_war[par] = [t_bf, t_sq2, tx2s]
            toks_hi = emit_fm_transpose(xln_t[par], fm_view(fm, r),
                                        [t_bf, tok_qkv_pe])
            txs = DSY("xl" + str(par), lambda sy, r=r, par=par: sy.dma_start(
                out=x2ln_dram[r * 128:(r + 1) * 128, :], in_=xln_t[par][:, :]),
                waits=[t_bf])
            tlo = V(lambda v, par=par: v.tensor_tensor(
                xsq[par][:, :], xlnf2[par][:, :], xln_t[par][:, :], ALU.subtract),
                waits=[t_bf])
            tlo_b = V(lambda v, par=par: v.tensor_copy(xln_t[par][:, :],
                                                       xsq[par][:, :]),
                      waits=[tlo, toks_hi[0][0], toks_hi[1][0], txs])
            lo_toks = emit_fm_transpose(
                xln_t[par],
                lambda half: lofm_r[:, :].rearrange("p (c l) -> p c l", c=8)[
                    :, half * 4:half * 4 + 4, :],
                [tlo_b])
            xlnt_last[par] = [lo_toks[0][0], lo_toks[1][0]]
            pp2 = 2 + r % 2
            lastmm = None
            nrmm = ECH * 3
            irm = 0
            for ec in range(ECH):
                for wt, xt_, isr in ((wrhi, fm, 0), (wrlo, fm, 0), (wrhi, lofm_r, 1)):
                    first = irm == 0
                    last = irm == nrmm - 1
                    def rmm(t, ec=ec, wt=wt, xt_=xt_, isr=isr, r=r, pp2=pp2,
                            first=first, last=last):
                        rhs = (xt_[:, ec * L + r * 128:ec * L + r * 128 + 128] if isr == 0
                               else xt_[:, ec * 128:(ec + 1) * 128])
                        return t.matmul(p_big[pp2][0:8, 0:128], wt[:, ec * 8:(ec + 1) * 8],
                                        rhs, start=first, stop=last)
                    lastmm = T(rmm, waits=[toks_hi[1][1], lo_toks[1][1], tok_wrlo,
                                           pbig_war[pp2]] if first else [])
                    irm += 1
            tsc = A(lambda a, r=r, pp2=pp2: a.activation(
                scores_sb[0:8, r * 128:(r + 1) * 128], p_big[pp2][0:8, 0:128],
                AF.Identity, bias=br_sb[0:8, 0:1]), waits=[lastmm, tok_smalldma])
            pbig_war[pp2] = tsc
            scores_tok[tt] = tsc
            r_done.append(tsc)
      tok_x2fm = ("dve", S.cnt["dve"])

      oh_war = None
      for r in range(NT):
        pp = r % 2
        ttr = T(lambda t, r=r, pp=pp: t.transpose(
            p_big[pp][0:128, 0:8], scores_sb[0:8, r * 128:(r + 1) * 128],
            ident_f[0:8, 0:8]), waits=[r_done[r], pbig_war[pp]])
        t1 = V(lambda v, pp=pp: v.reduce_max(ohtmp[:, 0:1], p_big[pp][0:128, 0:8],
                                             axis=AX.X), waits=[ttr])
        t2 = V(lambda v, pp=pp: v.tensor_scalar(ohtmp[:, 8:16], p_big[pp][0:128, 0:8],
                                                ohtmp[:, 0:1], None, ALU.is_ge),
               waits=[t1])
        pbig_war[pp] = t2
        t3 = V(lambda v: v.tensor_tensor(ohtmp[:, 16:24], ohtmp[:, 8:16], w8[:, :],
                                         ALU.mult), waits=[t2, tok_iota])
        t4 = V(lambda v: v.reduce_max(ohtmp[:, 1:2], ohtmp[:, 16:24], axis=AX.X),
               waits=[t3])
        t5 = V(lambda v: v.tensor_scalar(ohtmp[:, 2:3], ohtmp[:, 1:2], -1.0, 8.0,
                                         ALU.mult, ALU.add), waits=[t4])
        t6 = V(lambda v: v.tensor_scalar(oh_b[:, :], iota8[:, :], ohtmp[:, 2:3], None,
                                         ALU.is_equal), waits=[t5, oh_war])
        V(lambda v, r=r: v.tensor_copy(oh_all[:, r * 8:(r + 1) * 8], oh_b[:, :]),
          waits=[t6])
        tt2 = T(lambda t, pp=pp: t.transpose(p_tb[pp][0:8, 0:128], oh_b[:, :],
                                             ident_b[:, :]),
                waits=[t6, ptb_war[pp]])
        oh_war = tt2
        ptb_war[pp] = V(lambda v, r=r, pp=pp: v.tensor_copy(
            onehot_fm[0:8, r * 128:(r + 1) * 128], p_tb[pp][0:8, 0:128]), waits=[tt2])
      tok_router = ("dve", S.cnt["dve"])
    else:
      tok_x2fm = ("dve", S.cnt["dve"])
      tok_router = tok_x2fm

    if do_attn:
      CAP = 1024
      for r in range(NT):
        pp = 2 + r % 2
        lastmm = None
        for r2 in range(r + 1):
            lastmm = T(lambda t, r=r, r2=r2, pp=pp: t.matmul(
                p_big[pp][0:128, 0:8],
                tri_b[:, :] if r2 == r else onesq_b[:, :],
                oh_all[:, r2 * 8:(r2 + 1) * 8],
                start=(r2 == 0), stop=(r2 == r)),
                waits=[tok_router, tok_tri, tok_onesq, pbig_war[pp]] if r2 == 0 else [])
        tb1 = V(lambda v, r=r, pp=pp: v.tensor_tensor(
            ranktmp[:, 0:8], p_big[pp][0:128, 0:8], oh_all[:, r * 8:(r + 1) * 8],
            ALU.mult), waits=[lastmm])
        pbig_war[pp] = tb1
        tb2 = V(lambda v: v.reduce_sum(ranktmp[:, 8:9], ranktmp[:, 0:8], axis=AX.X),
                waits=[tb1])
        te1 = V(lambda v, r=r: v.tensor_tensor(
            ranktmp[:, 0:8], ecrow[:, :], oh_all[:, r * 8:(r + 1) * 8], ALU.mult),
            waits=[tb2, tok_ecrow])
        te2 = V(lambda v: v.reduce_sum(ranktmp[:, 9:10], ranktmp[:, 0:8], axis=AX.X),
                waits=[te1])
        tc1 = V(lambda v: v.tensor_copy(ranki[:, 0:1], ranktmp[:, 8:9]), waits=[tb2])
        tc2 = V(lambda v: v.tensor_scalar(ranki[:, 1:2], ranki[:, 0:1], 7, None,
                                          ALU.arith_shift_right), waits=[tc1])
        tc3 = V(lambda v: v.tensor_scalar(ranki[:, 2:3], ranki[:, 0:1], 10, None,
                                          ALU.arith_shift_right), waits=[tc1])
        tc4 = V(lambda v: v.tensor_copy(ranktmp[:, 10:11], ranki[:, 1:2]), waits=[tc2])
        tc5 = V(lambda v: v.tensor_copy(ranktmp[:, 11:12], ranki[:, 2:3]), waits=[tc3])
        tm1 = V(lambda v: v.scalar_tensor_tensor(ranktmp[:, 12:13], ranktmp[:, 10:11],
                                                 -128.0, ranktmp[:, 8:9],
                                                 ALU.mult, ALU.add), waits=[tc4])
        tq1 = V(lambda v: v.tensor_tensor(ranktmp[:, 13:14], ranktmp[:, 10:11],
                                          ranktmp[:, 9:10], ALU.add),
                waits=[tc4, te2])
        td1 = V(lambda v: v.scalar_tensor_tensor(ranktmp[:, 14:15], ranktmp[:, 12:13],
                                                 64.0, ranktmp[:, 13:14],
                                                 ALU.mult, ALU.add), waits=[tm1, tq1])
        tb4 = V(lambda v: v.scalar_tensor_tensor(ranktmp[:, 19:20], ranktmp[:, 11:12],
                                                 16384.0, ranktmp[:, 14:15], ALU.mult,
                                                 ALU.add), waits=[tc5, td1])
        V(lambda v, r=r: v.tensor_copy(dest_all[:, r:r + 1], ranktmp[:, 19:20]),
          waits=[tb4])
      tok_dest = ("dve", S.cnt["dve"])
      scat_last = [None, None]
      for r in range(NT):
          par = r % 2
          w0 = ([("c_xl0", S.cnt["c_xl0"]), ("c_xl1", S.cnt["c_xl1"])]
                if r < 2 else [])
          trl = DSY("xl" + str(par), lambda sy, r=r, par=par: sy.dma_start(
              out=xln_t[par][:, :], in_=x2ln_dram[r * 128:(r + 1) * 128, :]),
              waits=w0 + (xlnt_last[par] or []) + [scat_last[par]])
          scat_last[par] = DGP("gs" + str(par), lambda g, r=r, par=par:
              g.indirect_dma_start(
                  out=xcomp_dram[:, :],
                  out_offset=IndirectOffsetOnAxis(dest_all[:, r:r + 1], 0),
                  in_=xln_t[par][:, :], in_offset=None,
                  bounds_check=8 * 1024 - 1, oob_is_err=False),
              waits=[tok_dest, trl] + ([tok_xz] if r < 2 else []))
          xlnt_last[par] = None
      tok_xcomp = [("c_gs0", S.cnt["c_gs0"]), ("c_gs1", S.cnt["c_gs1"])]
    else:
      tok_xcomp = []
      CAP = 1024

    if "x2" in dump_d:
        DSY("dbg2", lambda sy: sy.dma_start(out=dump_d["x2"][:, :], in_=x2_d[0:128, :]),
            waits=[chan_total("x2s0"), chan_total("x2s1"), tok_router])
    if "scores" in dump_d:
        DSY("dbg2", lambda sy: sy.dma_start(out=dump_d["scores"][:, :], in_=scores_sb[0:8, :]),
            waits=[tok_router])
    if "onehot" in dump_d:
        DGP("dbg", lambda g: g.dma_start(out=dump_d["onehot"][:, :], in_=onehot_fm[0:8, :]),
            waits=[tok_router])

    S.barrier()
    ctxA.close()
    ctxE.close()
    h_full = sbF("h_full", [128, 16 * 1024], BF16)
    w1o = [sbF(f"w1o{i}", [128, ECH * 512], BF16) for i in range(2)]
    w2hc = [sbF(f"w2hc{i}", [128, 16 * 128], BF16) for i in range(2)]
    xg_tm = [sbF(f"xg_tm{i}", [128, DM], BF16) for i in range(4)]
    xgfm = sbF("xgfm", [128, ECH * 1024], BF16)
    yfm = sbF("yfm", [128, 8 * 1024], F32)
    ytm = [sbF(f"ytm{i}", [128, DM], F32) for i in range(2)]
    ytile = [sbF(f"ytile{i}", [128, DM], F32) for i in range(2)]

    if do_ffn:
      w1_tok = [None, None]
      w2_tok = [None, None]
      w1_war = [None, None]
      w2_war = [None, None]
      xg_war = [None, None, None, None]
      xgfm_war = None
      yfm_war = None
      ytm_war = [None, None]
      ysc_tok = [None, None]
      for e in range(NE):
        for b in range(8):
            col = e * 8 + b
            bp = b % 4
            tg_ = DSY("gx" + str(bp), lambda sy, col=col, bp=bp: sy.dma_start(
                out=xg_tm[bp][:, :].rearrange("p (c l) -> p c l", c=1),
                in_=xcomp_dram.rearrange("(p c) l -> p c l", c=64)[
                    :, col:col + 1, :]),
                waits=tok_xcomp + [xg_war[bp]])
            toks = emit_fm_transpose(
                xg_tm[bp],
                lambda hf, b=b: xgfm[:, :].rearrange("p (c l) -> p c l", c=ECH)[
                    :, hf * 4:hf * 4 + 4, b * 128:(b + 1) * 128],
                [tg_, xgfm_war])
            xg_war[bp] = toks[-1][0]
        tok_xg = ("dve", S.cnt["dve"])
        ev0 = {}
        lastw1 = None
        for fh in range(2):
            gelu_last = None
            for f8 in range(4):
                f8g = fh * 4 + f8
                wp = f8g % 2
                w1_tok[wp] = DGP("w1" + str(wp), lambda g, e=e, f8g=f8g, wp=wp: g.dma_start(
                    out=w1o[wp][:, :].rearrange("p (ec f) -> p ec f", f=512),
                    in_=w1_d[e, :, f8g * 512:(f8g + 1) * 512]
                        .rearrange("(ec p) f -> p ec f", p=128)),
                    waits=[w1_war[wp]])
                for fc in range(4):
                    fcl = f8 * 4 + fc
                    fcg = fh * 16 + fcl
                    for sg in range(2):
                        pp = (fcl * 2 + sg) % 2
                        lastmm = None
                        for ec in range(ECH):
                            lastmm = T(lambda t, fc=fc, ec=ec, sg=sg, wp=wp, pp=pp: t.matmul(
                                p_big[pp][0:128, 0:512],
                                w1o[wp][:, ec * 512 + fc * 128:ec * 512 + fc * 128 + 128],
                                xgfm[:, ec * 1024 + sg * 512:ec * 1024 + sg * 512 + 512],
                                start=(ec == 0), stop=(ec == ECH - 1)),
                                waits=[w1_tok[wp], pbig_war[pp], tok_xg] if ec == 0 else [])
                        w1_war[wp] = lastmm
                        lastw1 = lastmm
                        tg = A(lambda a, e=e, fcg=fcg, fcl=fcl, sg=sg, pp=pp: a.activation(
                            h_full[:, fcl * 1024 + sg * 512:fcl * 1024 + sg * 512 + 512],
                            p_big[pp][0:128, 0:512],
                            AF.Gelu, bias=b1_t[:, fcg * 8 + e:fcg * 8 + e + 1]),
                            waits=[lastmm, tok_b1t, w2_war[0], w2_war[1]])
                        pbig_war[pp] = tg
                        gelu_last = tg
            for hc in range(8):
                wp2 = hc % 2
                w2_tok[wp2] = DGP("w2" + str(wp2), lambda g, e=e, fh=fh, hc=hc, wp2=wp2: g.dma_start(
                    out=w2hc[wp2][:, :].rearrange("p (fc hh) -> p fc hh", hh=128),
                    in_=w2_d[e, fh * 2048:(fh + 1) * 2048, hc * 128:(hc + 1) * 128]
                        .rearrange("(fc p) hh -> p fc hh", p=128)),
                    waits=[w2_war[wp2]])
                for sg in range(2):
                    pp2 = 2 + (hc * 2 + sg) % 2
                    lastmm = None
                    for fc16 in range(16):
                        lastmm = T(lambda t, fc16=fc16, sg=sg, wp2=wp2, pp2=pp2: t.matmul(
                            p_big[pp2][0:128, 0:512],
                            w2hc[wp2][:, fc16 * 128:(fc16 + 1) * 128],
                            h_full[:, fc16 * 1024 + sg * 512:fc16 * 1024 + sg * 512 + 512],
                            start=(fc16 == 0), stop=(fc16 == 15)),
                            waits=[w2_tok[wp2], gelu_last, pbig_war[pp2]] if fc16 == 0 else [])
                    w2_war[wp2] = lastmm
                    if fh == 0:
                        tev = A(lambda a, e=e, hc=hc, sg=sg, pp2=pp2: a.activation(
                            yfm[:, hc * 1024 + sg * 512:hc * 1024 + sg * 512 + 512],
                            p_big[pp2][0:128, 0:512],
                            AF.Identity, bias=b2_t[:, hc * 8 + e:hc * 8 + e + 1]),
                            waits=[lastmm, tok_b1t, yfm_war])
                    else:
                        tev = V(lambda v, hc=hc, sg=sg, pp2=pp2: v.tensor_tensor(
                            yfm[:, hc * 1024 + sg * 512:hc * 1024 + sg * 512 + 512],
                            yfm[:, hc * 1024 + sg * 512:hc * 1024 + sg * 512 + 512],
                            p_big[pp2][0:128, 0:512], ALU.add),
                            waits=[lastmm, ev0[(hc, sg)]])
                    ev0[(hc, sg)] = tev
                    pbig_war[pp2] = tev
        xgfm_war = lastw1
        tok_yfm = ("dve", S.cnt["dve"])
        last_ttk = None
        for b in range(8):
            col = e * 8 + b
            bp = b % 2
            evs = []
            for hg in range(2):
                ttk = None
                for hcl in range(4):
                    hc = hg * 4 + hcl
                    ttk = T(lambda t, hc=hc, hcl=hcl, b=b, bp=bp: t.transpose(
                        p_big[bp][0:128, hcl * 128:(hcl + 1) * 128],
                        yfm[:, hc * 1024 + b * 128:hc * 1024 + b * 128 + 128],
                        ident_f[:, :]),
                        waits=[tok_yfm, pbig_war[bp], tok_identf] if hcl == 0 else [])
                last_ttk = ttk
                ev = V(lambda v, hg=hg, bp=bp: v.tensor_copy(
                    ytm[bp][:, hg * 512:(hg + 1) * 512].rearrange(
                        "p (c l) -> p c l", c=4),
                    p_big[bp][0:128, 0:512].rearrange("p (c l) -> p c l", c=4)),
                    waits=[ttk, ytm_war[bp]])
                pbig_war[bp] = ev
                evs.append(ev)
            ysc = DSY("ys" + str(bp), lambda sy, col=col, bp=bp: sy.dma_start(
                out=y_dram.rearrange("(p c) l -> p c l", c=64)[:, col:col + 1, :],
                in_=ytm[bp][:, :].rearrange("p (c l) -> p c l", c=1)),
                waits=evs + [ysc_tok[bp]])
            ysc_tok[bp] = ysc
            ytm_war[bp] = ysc
        yfm_war = last_ttk
      tok_y = [("c_ys0", S.cnt["c_ys0"]), ("c_ys1", S.cnt["c_ys1"])]
    else:
      tok_y = [("dve", S.cnt["dve"])]
    fout_war = [None, None]
    yl_war = [None, None]
    for r in range(NT):
        par = r % 2
        src_final = x2_d if do_attn else x_d
        w0 = (tok_y + [chan_total("x2s0"), chan_total("x2s1")]) if r < 2 else []
        xl = DSY("x" + str(par), lambda sy, r=r, par=par, src_final=src_final: sy.dma_start(
            out=x_tile[par][:, :], in_=src_final[r * 128:(r + 1) * 128, :]),
            waits=(w0 + (xt_war[par] or [])))
        if do_ffn:
            tz = G(lambda g, par=par: g.memset(ytile[par][:, :], 0.0),
                   waits=[yl_war[par]])
            yl = DGP("yl" + str(par), lambda g, r=r, par=par: g.indirect_dma_start(
                out=ytile[par][:, :], out_offset=None,
                in_=y_dram[:, :],
                in_offset=IndirectOffsetOnAxis(dest_all[:, r:r + 1], 0),
                bounds_check=8 * 1024 - 1, oob_is_err=False),
                waits=w0 + [tz, tok_dest])
        else:
            yl = DSY("yl" + str(par), lambda sy, r=r, par=par: sy.dma_start(
                out=ytile[par][:, :], in_=y_dram[r * 128:(r + 1) * 128, :]),
                waits=[yl_war[par]])
        ta = V(lambda v, par=par: v.tensor_tensor(
            out_tile[par][:, :], ytile[par][:, :], x_tile[par][:, :], ALU.add),
            waits=[xl, yl, fout_war[par]])
        xt_war[par] = [ta]
        yl_war[par] = ta
        fout_war[par] = DSY("outs" + str(par), lambda sy, r=r, par=par: sy.dma_start(
            out=out_d[r * 128:(r + 1) * 128, :], in_=out_tile[par][:, :]),
            waits=[ta])

    S.emit("sync", None, waits=[chan_total(c) for c in
        ("outs0", "outs1", "x2s0", "x2s1", "sg", "dbg", "dbg2", "qs0", "qs1", "vt0", "vt1")
        if ("c_" + c) in S.cnt])

    semh = {}
    for name in S.order:
        semh[name] = ctxP.enter_context(nc.semaphore(name))
    with ctxP, ctxF:
        block = ctxP.enter_context(nc.Block())

        def emit_engine(section, oplist):
            def body(eng):
                for fn, waits, inc in oplist:
                    for s_, v_ in waits:
                        eng.wait_ge(semh[s_], v_)
                    if fn is None:
                        continue
                    instr = fn(eng)
                    if inc is not None:
                        instr.then_inc(semh[inc[0]], inc[1])
            section(body)

        emit_engine(block.sync, S.ops["sync"])
        emit_engine(block.gpsimd, S.ops["gpsimd"])
        emit_engine(block.tensor, S.ops["tensor"])
        emit_engine(block.scalar, S.ops["scalar"])
        emit_engine(block.vector, S.ops["vector"])
    return nc


def _host_inputs(inputs):
    q = np.ascontiguousarray(inputs["L0_attn_q"], dtype=np.float32)
    common = {
        "wqkvT": np.ascontiguousarray(np.asarray(inputs["L0_attn_in_proj_weight"], dtype=np.float32).T),
        "bqkv": np.ascontiguousarray(inputs["L0_attn_in_proj_bias"], dtype=np.float32),
        "woutT": np.ascontiguousarray(np.asarray(inputs["L0_attn_out_w"], dtype=np.float32).T),
        "outb": np.ascontiguousarray(inputs["L0_attn_out_b"], dtype=np.float32),
        "relpeT": np.ascontiguousarray(np.asarray(inputs["L0_attn_rel_pe"], dtype=np.float32).T),
        "wr": np.ascontiguousarray(inputs["L0_ffn_wr"], dtype=np.float32),
        "br": np.ascontiguousarray(inputs["L0_ffn_br"], dtype=np.float32),
        "w1": np.ascontiguousarray(inputs["L0_ffn_w1"], dtype=np.float32),
        "b1": np.ascontiguousarray(inputs["L0_ffn_b1"], dtype=np.float32),
        "w2": np.ascontiguousarray(inputs["L0_ffn_w2"], dtype=np.float32),
        "b2": np.ascontiguousarray(inputs["L0_ffn_b2"], dtype=np.float32),
    }
    in_maps = []
    for b in range(8):
        m = dict(common)
        m["x"] = np.ascontiguousarray(q[b], dtype=np.float32)
        in_maps.append(m)
    return in_maps


def kernel(**inputs):
    in_maps = _host_inputs(inputs)
    nc = build()
    res = run_bass_kernel_spmd(nc, in_maps, core_ids=list(range(8)))
    out = np.stack([np.asarray(res.results[b]["out"]) for b in range(8)], axis=0)
    return out.astype(np.float32)
